# version 12
# baseline (speedup 1.0000x reference)
import numpy as np
import ml_dtypes
from contextlib import ExitStack

import concourse.bass as bass
import concourse.mybir as mybir
from concourse.bass_utils import run_bass_kernel_spmd

F32 = mybir.dt.float32
BF16 = mybir.dt.bfloat16
U8 = mybir.dt.uint8
I16 = mybir.dt.int16
AF = mybir.ActivationFunctionType
ALU = mybir.AluOpType
BF = ml_dtypes.bfloat16

S = 2048
D = 2048
NH, NKV, HD = 8, 4, 256
ROT = 128
WIN = 1024
NEG50 = -2.3819763e38 / 50.0
GA = 2.0 * 0.7978845608028654
GB = GA * 0.044715
MLP = 16384

ENGS = ("sp", "act", "dve", "pe", "pool")
_TICK = {"pe": "tpe", "dve": "tdve", "act": "tact", "pool": "tpool", "sp": "tsp"}
_DMAQ = {"sp": "dsp", "act": "dact", "pool": "dpool"}


class Prog:

    NLANES = 8

    def __init__(self, nc, ctx):
        self.nc = nc
        self.recs = {e: [] for e in ENGS}
        self.sems = {}
        names = ["tpe", "tdve", "tact", "tpool", "tsp"]
        names += [f"{q}{i}" for q in ("dsp", "dact", "dpool") for i in range(self.NLANES)]
        for name in names:
            self.sems[name] = ctx.enter_context(nc.semaphore(name))
        self.cnt = {k: 0 for k in self.sems}
        self.lane = {"dsp": 0, "dact": 0, "dpool": 0}
        self.waited = {e: {k: 0 for k in self.sems} for e in ENGS}
        self.bufs = {}

    def _buf(self, t):
        n = t if isinstance(t, str) else t.name
        return self.bufs.setdefault(n, {"w": None, "r": {}})

    def _need(self, reads, war):
        need = {}
        for t in reads:
            b = self._buf(t)
            if b["w"]:
                k, v = b["w"]
                need[k] = max(need.get(k, 0), v)
        for t in war:
            b = self._buf(t)
            if b["w"]:
                k, v = b["w"]
                need[k] = max(need.get(k, 0), v)
            for k, v in b["r"].items():
                need[k] = max(need.get(k, 0), v)
        return need

    def _edges(self, eng, need):
        out = []
        w = self.waited[eng]
        for k, v in need.items():
            if w[k] < v:
                out.append((k, v))
                w[k] = v
        return out

    def op(self, eng, build, reads=(), writes=(), war=()):
        need = self._need(tuple(reads), tuple(writes) + tuple(war))
        edges = self._edges(eng, need)
        tk = _TICK[eng]
        self.cnt[tk] += 1
        val = self.cnt[tk]
        sems = self.sems

        def emit(h, edges=edges, build=build, tk=tk):
            for k, v in edges:
                h.wait_ge(sems[k], v)
            build(h).then_inc(sems[tk], 1)

        self.recs[eng].append(emit)
        for t in writes:
            b = self._buf(t)
            b["w"] = (tk, val)
            b["r"] = {}
        for t in reads:
            self._buf(t)["r"][tk] = val
        return (tk, val)

    def dma(self, eng, out_ap, in_ap, reads=(), writes=()):
        need = self._need(tuple(reads), tuple(writes))
        q = _DMAQ[eng]
        ln = self.lane[q]
        self.lane[q] = (ln + 1) % self.NLANES
        dk = f"{q}{ln}"
        if self.cnt[dk] > 0:
            need[dk] = max(need.get(dk, 0), self.cnt[dk])
        edges = self._edges(eng, need)
        self.cnt[dk] += 16
        val = self.cnt[dk]
        sems = self.sems

        def emit(h, edges=edges, out_ap=out_ap, in_ap=in_ap, dk=dk):
            for k, v in edges:
                h.wait_ge(sems[k], v)
            h.dma_start(out=out_ap, in_=in_ap).then_inc(sems[dk], 16)

        self.recs[eng].append(emit)
        for t in writes:
            b = self._buf(t)
            b["w"] = (dk, val)
            b["r"] = {}
        for t in reads:
            self._buf(t)["r"][dk] = val
        return (dk, val)

    def mm_group(self, items, psum, extra_reads=()):
        psums = psum if isinstance(psum, (list, tuple)) else [psum]
        all_reads = {(t if isinstance(t, str) else t.name) for t in extra_reads}
        for _, rd in items:
            all_reads.update((t if isinstance(t, str) else t.name) for t in rd)
        need = self._need(tuple(all_reads), tuple(psums))
        edges = self._edges("pe", need)
        tk = "tpe"
        self.cnt[tk] += 1
        val = self.cnt[tk]
        sems = self.sems

        def emit(h, edges=edges, items=items, tk=tk):
            for k, v in edges:
                h.wait_ge(sems[k], v)
            last = None
            for build, _ in items:
                last = build(h)
            last.then_inc(sems[tk], 1)

        self.recs["pe"].append(emit)
        for ps_ in psums:
            b = self._buf(ps_)
            b["w"] = (tk, val)
            b["r"] = {}
        for t in all_reads:
            self._buf(t)["r"][tk] = val
        return (tk, val)

    def barrier(self):
        snap = dict(self.cnt)
        sems = self.sems
        for e in ENGS:
            w = self.waited[e]
            todo = [(k, v) for k, v in snap.items() if v > 0 and w[k] < v]
            for k, v in todo:
                w[k] = v

            def emit(h, todo=todo):
                for k, v in todo:
                    h.wait_ge(sems[k], v)
            self.recs[e].append(emit)

    def raw(self, eng, fn):
        self.recs[eng].append(fn)

    def emit_all(self):
        nc = self.nc
        recs = self.recs
        with nc.Block() as block:
            @block.sync
            def _(h):
                for f in recs["sp"]:
                    f(h)

            @block.scalar
            def _(h):
                for f in recs["act"]:
                    f(h)

            @block.vector
            def _(h):
                for f in recs["dve"]:
                    f(h)

            @block.tensor
            def _(h):
                for f in recs["pe"]:
                    f(h)

            @block.gpsimd
            def _(h):
                for f in recs["pool"]:
                    f(h)


def _build_program():
    nc = bass.Bass("TRN2", target_bir_lowering=False)

    def din(name, shape, dt):
        return nc.dram_tensor(name, list(shape), dt, kind="ExternalInput").ap()

    x_ap = din("x", (S, D), F32)
    wq_ap = din("wq", (D, NH * HD), F32)
    wk_ap = din("wk", (D, NKV * HD), F32)
    wv_ap = din("wv", (D, NKV * HD), F32)
    wo_ap = din("wo", (NH * HD, D), F32)
    qs_ap = din("qs", (128, HD), F32)
    ks_ap = din("ks", (128, HD), F32)
    gbt_ap = din("gbt", (128, 64 * 2048), U8)
    ubt_ap = din("ubt", (128, 64 * 2048), U8)
    dbt_ap = din("dbt", (128, 16 * 8192), U8)
    gsx_ap = din("gsx", (128, 64 * 2048), F32)
    usx_ap = din("usx", (128, 64 * 2048), F32)
    dsx_ap = din("dsx", (128, 16 * 8192), F32)
    cos_ap = din("cosb", (128, 16 * 64), BF16)
    sin_ap = din("sinb", (128, 16 * 64), BF16)
    m1_ap = din("m1", (128, 128), F32)
    m2_ap = din("m2", (128, 128), F32)
    id_ap = din("ident", (128, 128), BF16)
    out_ap = nc.dram_tensor("out", [S, D], F32, kind="ExternalOutput").ap()

    q_sc = nc.dram_tensor("q_sc", [S, D], BF16, kind="Internal").ap()
    ao_sc = nc.dram_tensor("ao_sc", [S, D], BF16, kind="Internal").ap()
    s_sc = nc.dram_tensor("s_sc", [S, D], F32, kind="Internal").ap()
    int_sc = nc.dram_tensor("int_sc", [MLP, S], BF16, kind="Internal").ap()

    root = ExitStack()
    with root:
        p = Prog(nc, root)

        def sb(ctx, name, shape, dt, side=None):
            if side:
                return ctx.enter_context(nc.sbuf_tensor(name, shape, dt, side=side))
            return ctx.enter_context(nc.sbuf_tensor(name, shape, dt))

        def ps(ctx, name, shape, dt):
            return ctx.enter_context(nc.psum_tensor(name, shape, dt))

        ctxK = ExitStack()
        ident = sb(ctxK, "ident_s", [128, 128], BF16, side="right")
        m1 = sb(ctxK, "m1_s", [128, 128], F32, side="right")
        m2 = sb(ctxK, "m2_s", [128, 128], F32, side="right")
        cosb = sb(ctxK, "cos_s", [128, 1024], BF16, side="right")
        sinb = sb(ctxK, "sin_s", [128, 1024], BF16, side="right")
        qsr = sb(ctxK, "qs_s", [128, HD], F32, side="right")
        ksr = sb(ctxK, "ks_s", [128, HD], F32, side="right")
        c_eps_ln = sb(ctxK, "c_eps_ln", [128, 1], F32, side="right")
        c_eps_q = sb(ctxK, "c_eps_q", [128, 1], F32, side="right")
        c_eps_k = sb(ctxK, "c_eps_k", [128, 1], F32, side="right")
        p.op("dve", lambda h: h.memset(c_eps_ln[:], 1e-5), writes=[c_eps_ln])
        p.op("dve", lambda h: h.memset(c_eps_q[:], 2.56e-4), writes=[c_eps_q])
        p.op("dve", lambda h: h.memset(c_eps_k[:], 1e-6), writes=[c_eps_k])
        for t, apx in ((ident, id_ap), (m1, m1_ap), (m2, m2_ap), (cosb, cos_ap),
                       (sinb, sin_ap), (qsr, qs_ap), (ksr, ks_ap)):
            p.dma("sp", t[:], apx, writes=[t])

        ctxA = ExitStack()
        xnT = [sb(ctxA, f"xnT{c}", [128, S], BF16) for c in range(16)]

        with ExitStack() as cl:
            xin = [sb(cl, f"ln_x{i}", [128, D], F32) for i in range(2)]
            sqs = [sb(cl, f"ln_sq{i}", [128, D], F32) for i in range(2)]
            xnt = [sb(cl, f"ln_xn{i}", [128, D], BF16) for i in range(2)]
            st1 = sb(cl, "ln_s1", [128, 2], F32)
            st2 = sb(cl, "ln_s2", [128, 2], F32)
            mean = sb(cl, "ln_mean", [128, 1], F32)
            var = sb(cl, "ln_var", [128, 1], F32)
            rstd = sb(cl, "ln_rstd", [128, 1], F32)
            ptr = [ps(cl, f"ln_ptr{i}", [128, 1024], BF16) for i in range(2)]
            for tt in range(16):
                xi = xin[tt % 2]
                sq = sqs[tt % 2]
                xo = xnt[tt % 2]
                p.dma("sp", xi[:], x_ap[128 * tt:128 * (tt + 1), :], writes=[xi])
                p.op("dve", lambda h, xi=xi: h.tensor_reduce(
                    out=st1[:, 0:1], in_=xi[:], axis=mybir.AxisListType.X, op=ALU.add),
                    reads=[xi], writes=[st1])
                p.op("act", lambda h, xi=xi, sq=sq: h.activation(
                    out=sq[:], in_=xi[:], func=AF.Square, accum_out=st2[:, 0:1]),
                    reads=[xi], writes=[sq, st2])
                p.op("dve", lambda h: h.tensor_scalar(
                    out=mean[:], in0=st1[:, 0:1], scalar1=1.0 / D, scalar2=None, op0=ALU.mult),
                    reads=[st1], writes=[mean])
                p.op("act", lambda h: h.activation(
                    out=var[:], in_=mean[:], func=AF.Square, scale=-1.0),
                    reads=[mean], writes=[var])
                p.op("dve", lambda h: h.scalar_tensor_tensor(
                    out=var[:], in0=st2[:, 0:1], scalar=1.0 / D, in1=var[:],
                    op0=ALU.mult, op1=ALU.subtract),
                    reads=[st2, var], writes=[var])
                p.op("act", lambda h: h.activation(
                    out=rstd[:], in_=var[:], func=AF.Sqrt, bias=c_eps_ln[:]),
                    reads=[var, c_eps_ln], writes=[rstd])
                p.op("dve", lambda h: h.reciprocal(out=rstd[:], in_=rstd[:]),
                     reads=[rstd], writes=[rstd])
                p.op("dve", lambda h, xi=xi, xo=xo: h.tensor_scalar(
                    out=xo[:], in0=xi[:], scalar1=mean[:], scalar2=rstd[:],
                    op0=ALU.subtract, op1=ALU.mult),
                    reads=[xi, mean, rstd], writes=[xo])
                for kc in range(16):
                    c = kc if kc < 8 else kc - 8
                    off = 256 * c + (0 if kc < 8 else 1)
                    pt = ptr[kc % 2]
                    p.op("pe", lambda h, xo=xo, off=off, pt=pt: h.transpose(
                        pt[:, 0:128], xo[:, off:off + 255:2], ident[:]),
                        reads=[xo, ident], writes=[pt])
                    p.op("dve", lambda h, pt=pt, kc=kc, tt=tt: h.tensor_copy(
                        out=xnT[kc][:, 128 * tt:128 * (tt + 1)], in_=pt[:, 0:128]),
                        reads=[pt], writes=[xnT[kc]])
            p.barrier()

        with ExitStack() as cq:
            wqs = [sb(cq, f"wq{c}", [128, D], BF16) for c in range(16)]
            for c in range(16):
                p.dma("pool", wqs[c][:], wq_ap[128 * c:128 * (c + 1), :], writes=[wqs[c]])
            psq2 = [ps(cq, f"psq{i}", [128, 2048], F32) for i in range(2)]
            qn = sb(cq, "qn", [128, D], F32)
            qt = [sb(cq, f"qt{i}", [128, D], BF16) for i in range(2)]
            ssq = sb(cq, "q_ssq", [128, 8], F32)
            rst = sb(cq, "q_rst", [128, 8], F32)
            sqsc = sb(cq, "q_sqsc", [128, D], F32)
            r0 = sb(cq, "q_r0", [128, 8 * 64], F32)
            r1 = sb(cq, "q_r1", [128, 8 * 64], F32)
            for tt in range(16):
                psq = psq2[tt % 2]
                items = []
                for kc in range(16):
                    for qb in range(4):
                        def bld(h, kc=kc, qb=qb, tt=tt, psq=psq):
                            return h.matmul(psq[:, 512 * qb:512 * (qb + 1)],
                                            lhsT=xnT[kc][:, 128 * tt:128 * (tt + 1)],
                                            rhs=wqs[kc][:, 512 * qb:512 * (qb + 1)],
                                            start=(kc == 0), stop=(kc == 15))
                        items.append((bld, (xnT[kc], wqs[kc])))
                p.mm_group(items, [f"{psq.name}_b{qb}" for qb in range(4)])
                for hh in range(8):
                    p.op("act", lambda h, hh=hh, psq=psq: h.activation(
                        out=sqsc[:, 256 * hh:256 * (hh + 1)],
                        in_=psq[:, 256 * hh:256 * (hh + 1)], func=AF.Square,
                        accum_out=ssq[:, hh:hh + 1]),
                        reads=[f"{psq.name}_b{hh // 2}"], writes=[sqsc, ssq])
                p.op("act", lambda h: h.activation(
                    out=rst[:], in_=ssq[:], func=AF.Sqrt, bias=c_eps_q[:]),
                    reads=[ssq, c_eps_q], writes=[rst])
                p.op("dve", lambda h: h.reciprocal(out=rst[:], in_=rst[:]),
                     reads=[rst], writes=[rst])
                for hh in range(8):
                    p.op("dve", lambda h, hh=hh, psq=psq: h.scalar_tensor_tensor(
                        out=qn[:, 256 * hh:256 * (hh + 1)],
                        in0=psq[:, 256 * hh:256 * (hh + 1)],
                        scalar=rst[:, hh:hh + 1], in1=qsr[:],
                        op0=ALU.mult, op1=ALU.mult),
                        reads=[f"{psq.name}_b{hh // 2}", rst, qsr], writes=[qn])
                qo = qt[tt % 2]
                c_t = cosb[:, 64 * tt:64 * (tt + 1)]
                s_t = sinb[:, 64 * tt:64 * (tt + 1)]
                x0 = qn[:].rearrange("p (h d) -> p h d", d=256)[:, :, 0:64]
                x1 = qn[:].rearrange("p (h d) -> p h d", d=256)[:, :, 64:128]
                cB = c_t.unsqueeze(1).broadcast_to([128, 8, 64])
                sB = s_t.unsqueeze(1).broadcast_to([128, 8, 64])
                r0v = r0[:].rearrange("p (h d) -> p h d", d=64)
                r1v = r1[:].rearrange("p (h d) -> p h d", d=64)
                qe = qo[:].rearrange("p (h d) -> p h d", d=256)[:, :, 0:64]
                qo_ = qo[:].rearrange("p (h d) -> p h d", d=256)[:, :, 64:128]
                qrest = qo[:].rearrange("p (h d) -> p h d", d=256)[:, :, 128:256]
                xrest = qn[:].rearrange("p (h d) -> p h d", d=256)[:, :, 128:256]
                p.op("dve", lambda h, x0=x0, cB=cB, r0v=r0v: h.tensor_tensor(
                    out=r0v, in0=x0, in1=cB, op=ALU.mult), reads=[qn, cosb], writes=[r0])
                p.op("dve", lambda h, x1=x1, sB=sB, r1v=r1v: h.tensor_tensor(
                    out=r1v, in0=x1, in1=sB, op=ALU.mult), reads=[qn, sinb], writes=[r1])
                p.op("dve", lambda h, r0v=r0v, r1v=r1v, qe=qe: h.tensor_tensor(
                    out=qe, in0=r0v, in1=r1v, op=ALU.subtract),
                    reads=[r0, r1], writes=[qo])
                p.op("dve", lambda h, x0=x0, sB=sB, r0v=r0v: h.tensor_tensor(
                    out=r0v, in0=x0, in1=sB, op=ALU.mult), reads=[qn, sinb], writes=[r0])
                p.op("dve", lambda h, x1=x1, cB=cB, r1v=r1v: h.tensor_tensor(
                    out=r1v, in0=x1, in1=cB, op=ALU.mult), reads=[qn, cosb], writes=[r1])
                p.op("dve", lambda h, r0v=r0v, r1v=r1v, qo_=qo_: h.tensor_tensor(
                    out=qo_, in0=r0v, in1=r1v, op=ALU.add),
                    reads=[r0, r1], writes=[qo])
                p.op("dve", lambda h, xrest=xrest, qrest=qrest: h.tensor_copy(
                    out=qrest, in_=xrest), reads=[qn], writes=[qo])
                p.dma("sp", q_sc[128 * tt:128 * (tt + 1), :], qo[:], reads=[qo],
                      writes=["q_sc"])
            p.barrier()

        ctxB = ExitStack()
        kT = [sb(ctxB, f"kT{c}", [128, S], BF16) for c in range(8)]
        Vt = [sb(ctxB, f"V{t}", [128, NKV * HD], BF16) for t in range(16)]

        with ExitStack() as ck:
            wks = [sb(ck, f"wk{c}", [128, NKV * HD], BF16) for c in range(16)]
            for c in range(16):
                p.dma("pool", wks[c][:], wk_ap[128 * c:128 * (c + 1), :], writes=[wks[c]])
            psk2 = [ps(ck, f"psk{i}", [128, 1024], F32) for i in range(2)]
            kn = sb(ck, "kn", [128, 1024], F32)
            kt_ = sb(ck, "kt_", [128, 1024], BF16)
            kssq = sb(ck, "k_ssq", [128, 4], F32)
            krst = sb(ck, "k_rst", [128, 4], F32)
            ksqsc = sb(ck, "k_sqsc", [128, 1024], F32)
            kr0 = sb(ck, "k_r0", [128, 4 * 64], F32)
            kr1 = sb(ck, "k_r1", [128, 4 * 64], F32)
            ptk = [ps(ck, f"ptk{i}", [128, 1024], BF16) for i in range(2)]
            for tt in range(16):
                psk = psk2[tt % 2]
                items = []
                for kc in range(16):
                    for kb in range(2):
                        def bld(h, kc=kc, kb=kb, tt=tt, psk=psk):
                            return h.matmul(psk[:, 512 * kb:512 * (kb + 1)],
                                            lhsT=xnT[kc][:, 128 * tt:128 * (tt + 1)],
                                            rhs=wks[kc][:, 512 * kb:512 * (kb + 1)],
                                            start=(kc == 0), stop=(kc == 15))
                        items.append((bld, (xnT[kc], wks[kc])))
                p.mm_group(items, [f"{psk.name}_b{kb}" for kb in range(2)])
                for hh in range(4):
                    p.op("act", lambda h, hh=hh, psk=psk: h.activation(
                        out=ksqsc[:, 256 * hh:256 * (hh + 1)],
                        in_=psk[:, 256 * hh:256 * (hh + 1)], func=AF.Square,
                        accum_out=kssq[:, hh:hh + 1]),
                        reads=[f"{psk.name}_b{hh // 2}"], writes=[ksqsc, kssq])
                p.op("act", lambda h: h.activation(
                    out=krst[:], in_=kssq[:], func=AF.Sqrt, scale=1.0 / 256, bias=c_eps_k[:]),
                    reads=[kssq, c_eps_k], writes=[krst])
                p.op("dve", lambda h: h.reciprocal(out=krst[:], in_=krst[:]),
                     reads=[krst], writes=[krst])
                for hh in range(4):
                    p.op("dve", lambda h, hh=hh, psk=psk: h.scalar_tensor_tensor(
                        out=kn[:, 256 * hh:256 * (hh + 1)],
                        in0=psk[:, 256 * hh:256 * (hh + 1)],
                        scalar=krst[:, hh:hh + 1], in1=ksr[:],
                        op0=ALU.mult, op1=ALU.mult),
                        reads=[f"{psk.name}_b{hh // 2}", krst, ksr], writes=[kn])
                c_t = cosb[:, 64 * tt:64 * (tt + 1)]
                s_t = sinb[:, 64 * tt:64 * (tt + 1)]
                x0 = kn[:].rearrange("p (h d) -> p h d", d=256)[:, :, 0:64]
                x1 = kn[:].rearrange("p (h d) -> p h d", d=256)[:, :, 64:128]
                cB = c_t.unsqueeze(1).broadcast_to([128, 4, 64])
                sB = s_t.unsqueeze(1).broadcast_to([128, 4, 64])
                r0v = kr0[:].rearrange("p (h d) -> p h d", d=64)
                r1v = kr1[:].rearrange("p (h d) -> p h d", d=64)
                ke = kt_[:].rearrange("p (h d) -> p h d", d=256)[:, :, 0:64]
                ko = kt_[:].rearrange("p (h d) -> p h d", d=256)[:, :, 64:128]
                krest = kt_[:].rearrange("p (h d) -> p h d", d=256)[:, :, 128:256]
                xrest = kn[:].rearrange("p (h d) -> p h d", d=256)[:, :, 128:256]
                p.op("dve", lambda h, x0=x0, cB=cB, r0v=r0v: h.tensor_tensor(
                    out=r0v, in0=x0, in1=cB, op=ALU.mult), reads=[kn, cosb], writes=[kr0])
                p.op("dve", lambda h, x1=x1, sB=sB, r1v=r1v: h.tensor_tensor(
                    out=r1v, in0=x1, in1=sB, op=ALU.mult), reads=[kn, sinb], writes=[kr1])
                p.op("dve", lambda h, r0v=r0v, r1v=r1v, ke=ke: h.tensor_tensor(
                    out=ke, in0=r0v, in1=r1v, op=ALU.subtract),
                    reads=[kr0, kr1], writes=[kt_])
                p.op("dve", lambda h, x0=x0, sB=sB, r0v=r0v: h.tensor_tensor(
                    out=r0v, in0=x0, in1=sB, op=ALU.mult), reads=[kn, sinb], writes=[kr0])
                p.op("dve", lambda h, x1=x1, cB=cB, r1v=r1v: h.tensor_tensor(
                    out=r1v, in0=x1, in1=cB, op=ALU.mult), reads=[kn, cosb], writes=[kr1])
                p.op("dve", lambda h, r0v=r0v, r1v=r1v, ko=ko: h.tensor_tensor(
                    out=ko, in0=r0v, in1=r1v, op=ALU.add),
                    reads=[kr0, kr1], writes=[kt_])
                p.op("dve", lambda h, xrest=xrest, krest=krest: h.tensor_copy(
                    out=krest, in_=xrest), reads=[kn], writes=[kt_])
                for dc in range(8):
                    pt = ptk[dc % 2]
                    p.op("pe", lambda h, dc=dc, pt=pt: h.transpose(
                        pt[:, 0:128], kt_[:, 128 * dc:128 * (dc + 1)], ident[:]),
                        reads=[kt_, ident], writes=[pt])
                    p.op("dve", lambda h, pt=pt, dc=dc, tt=tt: h.tensor_copy(
                        out=kT[dc][:, 128 * tt:128 * (tt + 1)], in_=pt[:, 0:128]),
                        reads=[pt], writes=[kT[dc]])
            p.barrier()

        with ExitStack() as cv:
            wvs = [sb(cv, f"wv{c}", [128, NKV * HD], BF16) for c in range(16)]
            for c in range(16):
                p.dma("pool", wvs[c][:], wv_ap[128 * c:128 * (c + 1), :], writes=[wvs[c]])
            psv2 = [ps(cv, f"psv{i}", [128, 1024], F32) for i in range(2)]
            for tt in range(16):
                psv = psv2[tt % 2]
                items = []
                for kc in range(16):
                    for kb in range(2):
                        def bld(h, kc=kc, kb=kb, tt=tt, psv=psv):
                            return h.matmul(psv[:, 512 * kb:512 * (kb + 1)],
                                            lhsT=xnT[kc][:, 128 * tt:128 * (tt + 1)],
                                            rhs=wvs[kc][:, 512 * kb:512 * (kb + 1)],
                                            start=(kc == 0), stop=(kc == 15))
                        items.append((bld, (xnT[kc], wvs[kc])))
                p.mm_group(items, [f"{psv.name}_b{kb}" for kb in range(2)])
                p.op("act", lambda h, tt=tt, psv=psv: h.activation(
                    out=Vt[tt][:], in_=psv[:], func=AF.Copy),
                    reads=[f"{psv.name}_b0", f"{psv.name}_b1"], writes=[Vt[tt]])
            p.barrier()

        with ExitStack() as ca:
            qin = [sb(ca, f"at_q{i}", [128, D], BF16) for i in range(2)]
            qTt = sb(ca, "at_qT", [128, D], BF16)
            sS = [sb(ca, f"at_sS{i}", [128, 1152], F32) for i in range(2)]
            pbf = [sb(ca, f"at_p{i}", [128, 1152], BF16) for i in range(2)]
            pTs = [sb(ca, f"at_pT{i}", [128, 1152], BF16) for i in range(2)]
            mx = sb(ca, "at_mx", [128, 2], F32)
            ngm = sb(ca, "at_ngm", [128, 2], F32)
            rsum = sb(ca, "at_rs", [128, 2], F32)
            rcp = sb(ca, "at_rcp", [128, 2], F32)
            aot = [sb(ca, f"at_ao{i}", [128, D], BF16) for i in range(2)]
            psS = [ps(ca, f"at_psS{i}", [128, 1536], F32) for i in range(2)]
            psT = ps(ca, "at_psT", [128, 1024], BF16)
            psO = ps(ca, "at_psO", [128, 512], F32)

            def scores(tt, hh, klo, w):
                hk = hh // 2
                pS = psS[hh % 2]
                items = []
                for dh in range(2):
                    o = 0
                    while o < w:
                        bw = min(512, w - o)
                        def bld(h, hh=hh, dh=dh, o=o, bw=bw, klo=klo, hk=hk,
                                pS=pS):
                            return h.matmul(
                                pS[:, o:o + bw],
                                lhsT=qTt[:, 128 * (2 * hh + dh):128 * (2 * hh + dh + 1)],
                                rhs=kT[2 * hk + dh][:, klo + o:klo + o + bw],
                                start=(dh == 0), stop=(dh == 1))
                        items.append((bld, (qTt, kT[2 * hk + dh])))
                        o += bw
                p.mm_group(items, [f"{pS.name}_{i}" for i in range((w + 511) // 512)])

            def softmax_pt_pv(tt, hh, klo, w, ao):
                par = hh % 2
                hk = hh // 2
                pS = psS[par]
                sSx = sS[par]
                pb = pbf[par]
                pT = pTs[par]
                nkt = w // 128
                nblk = (w + 511) // 512
                mxs = mx[:, par:par + 1]
                ngs = ngm[:, par:par + 1]
                rss = rsum[:, par:par + 1]
                rcs = rcp[:, par:par + 1]
                p.op("act", lambda h, w=w, pS=pS, sSx=sSx: h.activation(
                    out=sSx[:, 0:w], in_=pS[:, 0:w], func=AF.Tanh, scale=0.02),
                    reads=[f"{pS.name}_{i}" for i in range(nblk)], writes=[sSx])
                p.op("dve", lambda h, w=w, sSx=sSx: h.tensor_tensor(
                    out=sSx[:, w - 128:w], in0=sSx[:, w - 128:w], in1=m1[:],
                    op=ALU.add), reads=[sSx, m1], writes=[sSx])
                if nkt == 9:
                    p.op("dve", lambda h, sSx=sSx: h.tensor_tensor(
                        out=sSx[:, 0:128], in0=sSx[:, 0:128], in1=m2[:],
                        op=ALU.add), reads=[sSx, m2], writes=[sSx])
                p.op("dve", lambda h, w=w, sSx=sSx, mxs=mxs: h.tensor_reduce(
                    out=mxs, in_=sSx[:, 0:w], axis=mybir.AxisListType.X, op=ALU.max),
                    reads=[sSx], writes=["at_mx"])
                p.op("dve", lambda h, mxs=mxs, ngs=ngs: h.tensor_scalar(
                    out=ngs, in0=mxs, scalar1=-50.0, scalar2=None, op0=ALU.mult),
                    reads=["at_mx"], writes=["at_ngm"])
                p.op("act", lambda h, w=w, sSx=sSx, pb=pb, ngs=ngs, rss=rss:
                     h.activation(out=pb[:, 0:w], in_=sSx[:, 0:w], func=AF.Exp,
                                  scale=50.0, bias=ngs, accum_out=rss),
                     reads=[sSx, "at_ngm"], writes=[pb, "at_rs"])
                p.op("dve", lambda h, rss=rss, rcs=rcs: h.reciprocal(
                    out=rcs, in_=rss), reads=["at_rs"], writes=["at_rcp"])
                n1 = min(nkt, 8)
                for kb in range(n1):
                    p.op("pe", lambda h, kb=kb, pb=pb: h.transpose(
                        psT[:, 128 * kb:128 * (kb + 1)], pb[:, 128 * kb:128 * (kb + 1)],
                        ident[:]), reads=[pb, ident], writes=["at_psT"])
                p.op("act", lambda h, n1=n1, pT=pT: h.activation(
                    out=pT[:, 0:128 * n1], in_=psT[:, 0:128 * n1], func=AF.Copy),
                    reads=["at_psT"], writes=[pT])
                if nkt == 9:
                    p.op("pe", lambda h, pb=pb: h.transpose(
                        psT[:, 0:128], pb[:, 1024:1152], ident[:]),
                        reads=[pb, ident], writes=["at_psT"])
                    p.op("act", lambda h, pT=pT: h.activation(
                        out=pT[:, 1024:1152], in_=psT[:, 0:128], func=AF.Copy),
                        reads=["at_psT"], writes=[pT])
                items = []
                for kb in range(nkt):
                    kt_i = klo // 128 + kb
                    def bld(h, kb=kb, kt_i=kt_i, hk=hk, nkt=nkt, pT=pT):
                        return h.matmul(
                            psO[:, 0:256], lhsT=pT[:, 128 * kb:128 * (kb + 1)],
                            rhs=Vt[kt_i][:, 256 * hk:256 * (hk + 1)],
                            start=(kb == 0), stop=(kb == nkt - 1))
                    items.append((bld, (pT, Vt[kt_i])))
                p.mm_group(items, "at_psO")
                p.op("dve", lambda h, hh=hh, ao=ao, rcs=rcs: h.tensor_scalar(
                    out=ao[:, 256 * hh:256 * (hh + 1)], in0=psO[:, 0:256],
                    scalar1=rcs, scalar2=None, op0=ALU.mult),
                    reads=["at_psO", "at_rcp"], writes=[ao])

            for tt in range(16):
                qi = qin[tt % 2]
                ao = aot[tt % 2]
                p.dma("sp", qi[:], q_sc[128 * tt:128 * (tt + 1), :], reads=["q_sc"],
                      writes=[qi])
                for rnd in range(2):
                    for dc8 in range(8):
                        dc = 8 * rnd + dc8
                        p.op("pe", lambda h, qi=qi, dc=dc, dc8=dc8: h.transpose(
                            psT[:, 128 * dc8:128 * (dc8 + 1)],
                            qi[:, 128 * dc:128 * (dc + 1)], ident[:]),
                            reads=[qi, ident], writes=["at_psT"])
                    p.op("act", lambda h, rnd=rnd: h.activation(
                        out=qTt[:, 1024 * rnd:1024 * (rnd + 1)], in_=psT[:],
                        func=AF.Copy), reads=["at_psT"], writes=[qTt])
                klo = max(0, 128 * (tt - 8))
                w = 128 * (tt + 1) - klo
                scores(tt, 0, klo, w)
                for hh in range(8):
                    if hh + 1 < 8:
                        scores(tt, hh + 1, klo, w)
                    softmax_pt_pv(tt, hh, klo, w, ao)
                p.dma("sp", ao_sc[128 * tt:128 * (tt + 1), :], ao[:], reads=[ao],
                      writes=["ao_sc"])
            p.barrier()
        ctxB.close()
        p.barrier()

        with ExitStack() as cs:
            wos = [sb(cs, f"wo{c}", [128, D], BF16) for c in range(16)]
            for c in range(16):
                p.dma("pool", wos[c][:], wo_ap[128 * c:128 * (c + 1), :], writes=[wos[c]])
            aoin = [sb(cs, f"sp_ao{i}", [128, D], BF16) for i in range(2)]
            aoT = sb(cs, "sp_aoT", [128, D], BF16)
            sout = [sb(cs, f"sp_s{i}", [128, D], F32) for i in range(2)]
            pss2 = [ps(cs, f"sp_pss{i}", [128, 2048], F32) for i in range(2)]
            p.dma("sp", aoin[0][:], ao_sc[0:128, :], reads=["ao_sc"], writes=[aoin[0]])
            for tt in range(16):
                ai = aoin[tt % 2]
                so = sout[tt % 2]
                pss = pss2[tt % 2]
                if tt + 1 < 16:
                    p.dma("sp", aoin[(tt + 1) % 2][:],
                          ao_sc[128 * (tt + 1):128 * (tt + 2), :],
                          reads=["ao_sc"], writes=[aoin[(tt + 1) % 2]])
                pav = pss[:, 0:512].bitcast(BF16)
                tname = f"{pss.name}_T"
                for rnd in range(2):
                    for dc8 in range(8):
                        dc = 8 * rnd + dc8
                        p.op("pe", lambda h, ai=ai, dc=dc, dc8=dc8, pav=pav:
                             h.transpose(pav[:, 128 * dc8:128 * (dc8 + 1)],
                                         ai[:, 128 * dc:128 * (dc + 1)], ident[:]),
                             reads=[ai, ident], writes=[tname])
                    p.op("act", lambda h, rnd=rnd, pav=pav: h.activation(
                        out=aoT[:, 1024 * rnd:1024 * (rnd + 1)], in_=pav[:],
                        func=AF.Copy), reads=[tname], writes=[aoT])
                items = []
                for dc in range(16):
                    for hb in range(4):
                        def bld(h, dc=dc, hb=hb, pss=pss):
                            return h.matmul(pss[:, 512 * hb:512 * (hb + 1)],
                                            lhsT=aoT[:, 128 * dc:128 * (dc + 1)],
                                            rhs=wos[dc][:, 512 * hb:512 * (hb + 1)],
                                            start=(dc == 0), stop=(dc == 15))
                        items.append((bld, (aoT, wos[dc])))
                p.mm_group(items, [f"{pss.name}_{hb}" for hb in range(4)] + [tname])
                p.op("act", lambda h, so=so, pss=pss: h.activation(
                    out=so[:], in_=pss[:], func=AF.Copy),
                    reads=[f"{pss.name}_{hb}" for hb in range(4)], writes=[so])
                p.dma("sp", s_sc[128 * tt:128 * (tt + 1), :], so[:], reads=[so],
                      writes=["s_sc"])
            p.barrier()
        ctxK.close()
        p.barrier()

        NBW = 256
        NGRP = MLP // NBW
        with ExitStack() as cg:
            pk = {m: [sb(cg, f"gu_pk_{m}{i}", [128, 8 * NBW], U8) for i in range(2)]
                  for m in ("g", "u")}
            scx = {m: [sb(cg, f"gu_sc_{m}{i}", [128, 8 * NBW], BF16) for i in range(2)]
                   for m in ("g", "u")}
            cch = {m: [sb(cg, f"gu_C{m}{b}", [128, 16 * NBW], BF16)
                       for b in range(2)] for m in ("g", "u")}
            t7c = sb(cg, "gu_t7", [128, 16 * NBW], U8)
            s8c = sb(cg, "gu_s8", [128, 16 * NBW], U8)
            mgc = sb(cg, "gu_mg", [128, 16 * NBW], BF16)
            sgc = sb(cg, "gu_sg", [128, 16 * NBW], BF16)
            gbuf = [sb(cg, f"gu_gb{i}", [128, 2048], F32) for i in range(2)]
            ubuf = [sb(cg, f"gu_ub{i}", [128, 2048], F32) for i in range(2)]
            gsq = sb(cg, "gu_gsq", [128, 2048], F32)
            zb = sb(cg, "gu_z", [128, 2048], F32)
            itt = [sb(cg, f"gu_it{i}", [128, 2048], BF16) for i in range(2)]
            pgu = [ps(cg, f"gu_ps{i}", [128, 512], F32) for i in range(8)]

            def decode(mat, g):
                pkt = pk[mat][g % 2]
                sct = scx[mat][g % 2]
                cc = cch[mat][g % 2]
                p16 = pkt[:].bitcast(I16)
                tlo = t7c[:, 0:2048].bitcast(I16)
                thi = t7c[:, 2048:4096].bitcast(I16)
                slo = s8c[:, 0:2048].bitcast(I16)
                shi = s8c[:, 2048:4096].bitcast(I16)
                p.op("dve", lambda h, p16=p16, tlo=tlo: h.tensor_scalar(
                    out=tlo, in0=p16, scalar1=0x0707, scalar2=None,
                    op0=ALU.bitwise_and), reads=[pkt], writes=[t7c])
                p.op("dve", lambda h, p16=p16, thi=thi: h.tensor_scalar(
                    out=thi, in0=p16, scalar1=4, scalar2=0x0707,
                    op0=ALU.logical_shift_right, op1=ALU.bitwise_and),
                    reads=[pkt], writes=[t7c])
                p.op("dve", lambda h, p16=p16, slo=slo: h.tensor_scalar(
                    out=slo, in0=p16, scalar1=0x0808, scalar2=None,
                    op0=ALU.bitwise_and), reads=[pkt], writes=[s8c])
                p.op("dve", lambda h, p16=p16, shi=shi: h.tensor_scalar(
                    out=shi, in0=p16, scalar1=4, scalar2=0x0808,
                    op0=ALU.logical_shift_right, op1=ALU.bitwise_and),
                    reads=[pkt], writes=[s8c])
                bb16 = mgc[:].bitcast(I16)
                p.op("act", lambda h, bb16=bb16: h.activation(
                    out=bb16, in_=t7c[:], func=AF.Copy, scale=64.0, bias=16128.0),
                    reads=[t7c], writes=[mgc])
                p.op("act", lambda h: h.activation(
                    out=sgc[:], in_=s8c[:], func=AF.Copy, scale=-0.25, bias=1.0),
                    reads=[s8c], writes=[sgc])
                p.op("dve", lambda h: h.scalar_tensor_tensor(
                    out=mgc[:], in0=mgc[:], scalar=1.0, in1=mgc[:],
                    op0=ALU.min, op1=ALU.add), reads=[mgc], writes=[mgc])
                p.op("dve", lambda h: h.tensor_scalar(
                    out=mgc[:], in0=mgc[:], scalar1=-1.0, scalar2=None,
                    op0=ALU.add), reads=[mgc], writes=[mgc])
                p.op("dve", lambda h: h.tensor_tensor(
                    out=mgc[:], in0=mgc[:], in1=sgc[:], op=ALU.mult),
                    reads=[mgc, sgc], writes=[mgc])
                scb = sct[:].unsqueeze(1).broadcast_to([128, 2, 2048])
                p.op("dve", lambda h, cc=cc, scb=scb: h.tensor_tensor(
                    out=cc[:].rearrange("p (pl n) -> p pl n", pl=2),
                    in0=mgc[:].rearrange("p (pl n) -> p pl n", pl=2),
                    in1=scb, op=ALU.mult),
                    reads=[mgc, sct], writes=[cc])

            def load_group(mat, ap_w, ap_s, g):
                pkt = pk[mat][g % 2]
                sct = scx[mat][g % 2]
                p.dma("sp", pkt[:], ap_w[:, 2048 * g:2048 * (g + 1)],
                      writes=[pkt])
                p.dma("pool", sct[:], ap_s[:, 2048 * g:2048 * (g + 1)],
                      writes=[sct])

            MATS = (("g", gbt_ap, gsx_ap), ("u", ubt_ap, usx_ap))
            for mat, ap_w, ap_s in MATS:
                load_group(mat, ap_w, ap_s, 0)
                decode(mat, 0)
            def gelu_emit(g, nt):
                cglob = 2 * g + nt
                io = itt[nt]
                gb = gbuf[nt]
                ub = ubuf[nt]
                p.op("act", lambda h, gb=gb: h.activation(
                    out=gsq[:], in_=gb[:], func=AF.Square),
                    reads=[gb], writes=[gsq])
                p.op("dve", lambda h: h.tensor_scalar(
                    out=gsq[:], in0=gsq[:], scalar1=GB, scalar2=GA,
                    op0=ALU.mult, op1=ALU.add), reads=[gsq], writes=[gsq])
                p.op("dve", lambda h, gb=gb: h.tensor_tensor(
                    out=zb[:], in0=gsq[:], in1=gb[:], op=ALU.mult),
                    reads=[gsq, gb], writes=[zb])
                p.op("act", lambda h: h.activation(
                    out=gsq[:], in_=zb[:], func=AF.Sigmoid),
                    reads=[zb], writes=[gsq])
                p.op("dve", lambda h, gb=gb: h.tensor_tensor(
                    out=zb[:], in0=gb[:], in1=gsq[:], op=ALU.mult),
                    reads=[gb, gsq], writes=[zb])
                p.op("pool", lambda h, io=io, ub=ub: h.tensor_tensor(
                    out=io[:], in0=zb[:], in1=ub[:], op=ALU.mult),
                    reads=[zb, ub], writes=[io])
                p.dma("sp", int_sc[128 * cglob:128 * (cglob + 1), :], io[:],
                      reads=[io], writes=["int_sc"])

            pending = None
            for g in range(NGRP):
                buf = g % 2
                if g + 1 < NGRP:
                    for mat, ap_w, ap_s in MATS:
                        load_group(mat, ap_w, ap_s, g + 1)
                        decode(mat, g + 1)
                if pending is not None:
                    gelu_emit(*pending)
                    pending = None
                for nt in range(2):
                    gb = gbuf[nt]
                    ub = ubuf[nt]
                    boff = 4 * nt
                    for mat, obuf in (("g", gb), ("u", ub)):
                        cc = cch[mat][buf]
                        items = []
                        for kc in range(16):
                            pl = 0 if kc < 8 else 1
                            jc = kc % 8
                            for tb in range(4):
                                bank = pgu[boff + tb]
                                def bld(h, cc=cc, jc=jc, pl=pl, nt=nt, tb=tb,
                                        bank=bank, kc=kc):
                                    o = 2048 * pl + NBW * jc + 128 * nt
                                    return h.matmul(
                                        bank[:], lhsT=cc[:, o:o + 128],
                                        rhs=xnT[kc][:, 512 * tb:512 * (tb + 1)],
                                        start=(kc == 0), stop=(kc == 15))
                                items.append((bld, (cc, xnT[kc])))
                        p.mm_group(items, [pgu[boff + tb].name for tb in range(4)])
                        for tb in range(4):
                            bank = pgu[boff + tb]
                            p.op("act", lambda h, bank=bank, tb=tb, obuf=obuf:
                                 h.activation(out=obuf[:, 512 * tb:512 * (tb + 1)],
                                              in_=bank[:], func=AF.Copy),
                                 reads=[bank.name], writes=[obuf])
                    if nt == 0:
                        gelu_emit(g, 0)
                    else:
                        pending = (g, 1)
            gelu_emit(NGRP - 1, 1)
            p.barrier()
        ctxA.close()
        p.barrier()

        with ExitStack() as cd:
            pkd = [sb(cd, f"dn_pk{i}", [128, 4096], U8) for i in range(2)]
            scd = [sb(cd, f"dn_sc{i}", [128, 4096], BF16) for i in range(2)]
            cdh = [sb(cd, f"dn_C{x}", [128, 4096], BF16) for x in range(3)]
            itr = [sb(cd, f"dn_it{i}", [128, 2048], BF16) for i in range(8)]
            dt7 = sb(cd, "dn_t7", [128, 4096], U8)
            ds8 = sb(cd, "dn_s8", [128, 4096], U8)
            dbb = sb(cd, "dn_bb", [128, 4096], I16)
            dmg = sb(cd, "dn_mg", [128, 4096], BF16)
            dsg = sb(cd, "dn_sg", [128, 4096], BF16)
            acc = [sb(cd, f"acc{t}", [128, 1024], F32) for t in range(16)]
            xf = [sb(cd, f"fn_x{i}", [128, 1024], F32) for i in range(2)]
            sf = [sb(cd, f"fn_s{i}", [128, 1024], F32) for i in range(2)]
            gf = [sb(cd, f"fn_g{i}", [128, 1024], F32) for i in range(2)]
            pdn = [ps(cd, f"dn_ps{i}", [128, 1024], F32) for i in range(4)]

            def dn_load(ss, half):
                pp = pkd[ss % 2]
                sc = scd[ss % 2]
                base = ((ss % 16) * 2 + half) * 4096
                p.dma("pool", pp[:], dbt_ap[:, base:base + 4096], writes=[pp])
                p.dma("pool", sc[:],
                      dsx_ap[:, 8192 * (ss % 16) + 4096 * half:
                             8192 * (ss % 16) + 4096 * (half + 1)],
                      writes=[sc])

            def dn_decode(ss, slot):
                lo_plane = ss < 16
                pp = pkd[ss % 2]
                sc = scd[ss % 2]
                dst = cdh[slot]
                s16 = pp[:].bitcast(I16)
                t16 = dt7[:].bitcast(I16)
                x16 = ds8[:].bitcast(I16)
                if lo_plane:
                    p.op("dve", lambda h, s16=s16, t16=t16: h.tensor_scalar(
                        out=t16, in0=s16, scalar1=0x0707, scalar2=None,
                        op0=ALU.bitwise_and), reads=[pp], writes=[dt7])
                    p.op("dve", lambda h, s16=s16, x16=x16: h.tensor_scalar(
                        out=x16, in0=s16, scalar1=0x0808, scalar2=None,
                        op0=ALU.bitwise_and), reads=[pp], writes=[ds8])
                else:
                    p.op("dve", lambda h, s16=s16, t16=t16: h.tensor_scalar(
                        out=t16, in0=s16, scalar1=4, scalar2=0x0707,
                        op0=ALU.logical_shift_right, op1=ALU.bitwise_and),
                        reads=[pp], writes=[dt7])
                    p.op("dve", lambda h, s16=s16, x16=x16: h.tensor_scalar(
                        out=x16, in0=s16, scalar1=4, scalar2=0x0808,
                        op0=ALU.logical_shift_right, op1=ALU.bitwise_and),
                        reads=[pp], writes=[ds8])
                p.op("act", lambda h: h.activation(
                    out=dbb[:], in_=dt7[:], func=AF.Copy, scale=64.0, bias=16128.0),
                    reads=[dt7], writes=[dbb])
                p.op("act", lambda h: h.activation(
                    out=dsg[:], in_=ds8[:], func=AF.Copy, scale=-0.25, bias=1.0),
                    reads=[ds8], writes=[dsg])
                ab = dbb[:].bitcast(BF16)
                p.op("dve", lambda h, ab=ab: h.tensor_scalar(
                    out=dmg[:], in0=ab, scalar1=1.0, scalar2=None,
                    op0=ALU.min), reads=[dbb], writes=[dmg])
                p.op("dve", lambda h, ab=ab: h.tensor_tensor(
                    out=dmg[:], in0=dmg[:], in1=ab, op=ALU.add),
                    reads=[dmg, dbb], writes=[dmg])
                p.op("dve", lambda h: h.tensor_scalar(
                    out=dmg[:], in0=dmg[:], scalar1=-1.0, scalar2=None,
                    op0=ALU.add), reads=[dmg], writes=[dmg])
                p.op("dve", lambda h: h.tensor_tensor(
                    out=dmg[:], in0=dmg[:], in1=dsg[:], op=ALU.mult),
                    reads=[dmg, dsg], writes=[dmg])
                p.op("pool", lambda h, dst=dst, sc=sc: h.tensor_tensor(
                    out=dst[:], in0=dmg[:], in1=sc[:], op=ALU.mult),
                    reads=[dmg, sc], writes=[dst])

            def dn_iload(ss):
                for ns in range(4):
                    cglob = 4 * ss + ns
                    slot = itr[cglob % 8]
                    p.dma("sp", slot[:], int_sc[128 * cglob:128 * (cglob + 1), :],
                          reads=["int_sc"], writes=[slot])

            def dn_mm(ss, slot):
                cx = cdh[slot]
                for tt in range(16):
                    bank = pdn[tt % 4]
                    items = []
                    for ns in range(4):
                        cglob = 4 * ss + ns
                        sl = itr[cglob % 8]
                        for hb in range(2):
                            def bld(h, tt=tt, ns=ns, hb=hb, bank=bank, sl=sl,
                                    cx=cx):
                                return h.matmul(
                                    bank[:, 512 * hb:512 * (hb + 1)],
                                    lhsT=sl[:, 128 * tt:128 * (tt + 1)],
                                    rhs=cx[:, 1024 * ns + 512 * hb:
                                           1024 * ns + 512 * (hb + 1)],
                                    start=(ns == 0), stop=(ns == 3))
                            items.append((bld, (sl, cx)))
                    p.mm_group(items, bank.name)
                    if ss == 0:
                        p.op("dve", lambda h, bank=bank, tt=tt: h.tensor_copy(
                            out=acc[tt][:], in_=bank[:]),
                            reads=[bank.name], writes=[acc[tt]])
                    else:
                        p.op("dve", lambda h, bank=bank, tt=tt: h.tensor_tensor(
                            out=acc[tt][:], in0=acc[tt][:], in1=bank[:],
                            op=ALU.add),
                            reads=[bank.name, acc[tt]], writes=[acc[tt]])

            def finalize(half):
                for tt in range(16):
                    xi = xf[tt % 2]
                    si = sf[tt % 2]
                    gi = gf[tt % 2]
                    p.dma("act", xi[:],
                          x_ap[128 * tt:128 * (tt + 1),
                               1024 * half:1024 * (half + 1)], writes=[xi])
                    p.dma("act", si[:],
                          s_sc[128 * tt:128 * (tt + 1),
                               1024 * half:1024 * (half + 1)],
                          reads=["s_sc"], writes=[si])
                    p.op("act", lambda h, tt=tt, gi=gi: h.activation(
                        out=gi[:], in_=acc[tt][:], func=AF.Silu),
                        reads=[acc[tt]], writes=[gi])
                    p.op("dve", lambda h, gi=gi, si=si: h.tensor_tensor(
                        out=gi[:], in0=gi[:], in1=si[:], op=ALU.mult),
                        reads=[gi, si], writes=[gi])
                    p.op("dve", lambda h, gi=gi, xi=xi: h.tensor_tensor(
                        out=gi[:], in0=gi[:], in1=xi[:], op=ALU.add),
                        reads=[gi, xi], writes=[gi])
                    p.dma("act",
                          out_ap[128 * tt:128 * (tt + 1),
                                 1024 * half:1024 * (half + 1)], gi[:],
                          reads=[gi], writes=["out_hbm"])

            for half in range(2):
                dn_iload(0)
                dn_load(0, half)
                dn_decode(0, (32 * half) % 3)
                if half == 1:
                    finalize(0)
                for ss in range(32):
                    step = 32 * half + ss
                    if ss + 1 < 32:
                        dn_load(ss + 1, half)
                        dn_decode(ss + 1, (step + 1) % 3)
                        dn_iload(ss + 1)
                    dn_mm(ss, step % 3)
            finalize(1)
            p.barrier()

        p.emit_all()
    return nc


def _host_layout(inputs):
    x = np.asarray(inputs["L0_attn_x"])
    Wq = np.asarray(inputs["L0_attn_Wq"])
    Wk = np.asarray(inputs["L0_attn_Wk"])
    Wv = np.asarray(inputs["L0_attn_Wv"])
    Wo = np.asarray(inputs["L0_attn_Wo"])
    q_scale = np.asarray(inputs["L0_attn_q_scale"])
    k_scale = np.asarray(inputs["L0_attn_k_scale"])
    gW = np.asarray(inputs["L0_ffn_gate_W_packed"])
    gS = np.asarray(inputs["L0_ffn_gate_scales"])
    uW = np.asarray(inputs["L0_ffn_up_W_packed"])
    uS = np.asarray(inputs["L0_ffn_up_scales"])
    dW = np.asarray(inputs["L0_ffn_down_W_packed"])
    dS = np.asarray(inputs["L0_ffn_down_scales"])

    rho = np.concatenate([np.arange(0, D, 2), np.arange(1, D, 2)])
    pi = np.concatenate([np.arange(0, ROT, 2), np.arange(1, ROT, 2),
                         np.arange(ROT, HD)])
    piq = np.concatenate([pi + HD * h for h in range(NH)])
    pik = np.concatenate([pi + HD * h for h in range(NKV)])
    sig = np.concatenate([np.arange(0, MLP, 2), np.arange(1, MLP, 2)])

    shared = {}
    shared["wq"] = np.ascontiguousarray(Wq[rho][:, piq], dtype=np.float32)
    shared["wk"] = np.ascontiguousarray(Wk[rho][:, pik], dtype=np.float32)
    shared["wv"] = np.ascontiguousarray(Wv[rho], dtype=np.float32)
    shared["wo"] = np.ascontiguousarray(Wo, dtype=np.float32)
    shared["qs"] = np.ascontiguousarray(
        np.broadcast_to(q_scale[pi].astype(np.float32), (128, HD)))
    shared["ks"] = np.ascontiguousarray(
        np.broadcast_to(k_scale[pi].astype(np.float32), (128, HD)))

    def bytes_of(Wp):
        return Wp.view(np.uint8).reshape(Wp.shape[0], Wp.shape[1], 4)[:, :, 0]

    def gu_pack(Wp):
        B = bytes_of(Wp)[sig].T
        return np.ascontiguousarray(
            B.reshape(8, 128, 64, 256).transpose(1, 2, 0, 3).reshape(128, 64 * 2048))
    shared["gbt"] = gu_pack(gW)
    shared["ubt"] = gu_pack(uW)
    Dt = bytes_of(dW).T
    shared["dbt"] = np.ascontiguousarray(
        Dt.reshape(16, 4, 128, 2, 1024).transpose(2, 0, 3, 1, 4).reshape(128, 16 * 8192))
    p16 = np.arange(128) // 16
    def gu_exp(Ssc):
        gsT = Ssc[sig].T.astype(np.float32)
        rows = 8 * np.arange(8)[None, :] + p16[:, None]
        B1 = gsT[rows]
        return np.ascontiguousarray(
            B1.reshape(128, 8, 64, 256).transpose(0, 2, 1, 3).reshape(128, 64 * 2048))
    shared["gsx"] = gu_exp(gS)
    shared["usx"] = gu_exp(uS)
    dsT = dS.T.astype(np.float32)
    rows = (32 * np.arange(16)[None, :, None] + 8 * np.arange(4)[None, None, :]
            + p16[:, None, None])
    D1 = dsT[rows]
    D1 = D1.reshape(128, 16, 4, 2, 1024).transpose(0, 1, 3, 2, 4)
    shared["dsx"] = np.ascontiguousarray(D1.reshape(128, 16 * 8192))

    theta = 1.0 / (10000.0 ** (np.arange(0, ROT, 2, dtype=np.float32) / ROT))
    pos = np.arange(S, dtype=np.float32)
    ang = pos[:, None] * theta[None, :]
    cosc = np.cos(ang).astype(np.float32).reshape(16, 128, 64).transpose(1, 0, 2)
    sinc = np.sin(ang).astype(np.float32).reshape(16, 128, 64).transpose(1, 0, 2)
    shared["cosb"] = np.ascontiguousarray(cosc.reshape(128, 1024)).astype(BF)
    shared["sinb"] = np.ascontiguousarray(sinc.reshape(128, 1024)).astype(BF)
    ii = np.arange(128)[:, None]
    jj = np.arange(128)[None, :]
    shared["m1"] = np.where(jj <= ii, 0.0, NEG50).astype(np.float32)
    shared["m2"] = np.where(jj > ii, 0.0, NEG50).astype(np.float32)
    shared["ident"] = np.eye(128, dtype=BF)

    in_maps = []
    for c in range(8):
        m = dict(shared)
        m["x"] = np.ascontiguousarray(x[c], dtype=np.float32)
        in_maps.append(m)
    return in_maps


def kernel(**inputs):
    in_maps = _host_layout(inputs)
    nc = _build_program()
    res = run_bass_kernel_spmd(nc, in_maps, core_ids=list(range(8)))
    out = np.stack([np.asarray(r["out"]) for r in res.results])
    return out.astype(np.float32)


# version 13
# speedup vs baseline: 1.0750x; 1.0750x over previous
import numpy as np
import ml_dtypes
from contextlib import ExitStack

import concourse.bass as bass
import concourse.mybir as mybir
from concourse.bass_utils import run_bass_kernel_spmd

F32 = mybir.dt.float32
BF16 = mybir.dt.bfloat16
U8 = mybir.dt.uint8
I16 = mybir.dt.int16
AF = mybir.ActivationFunctionType
ALU = mybir.AluOpType
BF = ml_dtypes.bfloat16

S = 2048
D = 2048
NH, NKV, HD = 8, 4, 256
ROT = 128
WIN = 1024
NEG50 = -2.3819763e38 / 50.0
GA = 2.0 * 0.7978845608028654
GB = GA * 0.044715
MLP = 16384

ENGS = ("sp", "act", "dve", "pe", "pool")
_TICK = {"pe": "tpe", "dve": "tdve", "act": "tact", "pool": "tpool", "sp": "tsp"}
_DMAQ = {"sp": "dsp", "act": "dact", "pool": "dpool"}


class Prog:

    NLANES = 8

    def __init__(self, nc, ctx):
        self.nc = nc
        self.recs = {e: [] for e in ENGS}
        self.sems = {}
        names = ["tpe", "tdve", "tact", "tpool", "tsp"]
        names += [f"{q}{i}" for q in ("dsp", "dact", "dpool") for i in range(self.NLANES)]
        for name in names:
            self.sems[name] = ctx.enter_context(nc.semaphore(name))
        self.cnt = {k: 0 for k in self.sems}
        self.lane = {"dsp": 0, "dact": 0, "dpool": 0}
        self.waited = {e: {k: 0 for k in self.sems} for e in ENGS}
        self.bufs = {}

    def _buf(self, t):
        n = t if isinstance(t, str) else t.name
        return self.bufs.setdefault(n, {"w": None, "r": {}})

    def _need(self, reads, war):
        need = {}
        for t in reads:
            b = self._buf(t)
            if b["w"]:
                k, v = b["w"]
                need[k] = max(need.get(k, 0), v)
        for t in war:
            b = self._buf(t)
            if b["w"]:
                k, v = b["w"]
                need[k] = max(need.get(k, 0), v)
            for k, v in b["r"].items():
                need[k] = max(need.get(k, 0), v)
        return need

    def _edges(self, eng, need):
        out = []
        w = self.waited[eng]
        for k, v in need.items():
            if w[k] < v:
                out.append((k, v))
                w[k] = v
        return out

    def op(self, eng, build, reads=(), writes=(), war=()):
        need = self._need(tuple(reads), tuple(writes) + tuple(war))
        edges = self._edges(eng, need)
        tk = _TICK[eng]
        self.cnt[tk] += 1
        val = self.cnt[tk]
        sems = self.sems

        def emit(h, edges=edges, build=build, tk=tk):
            for k, v in edges:
                h.wait_ge(sems[k], v)
            build(h).then_inc(sems[tk], 1)

        self.recs[eng].append(emit)
        for t in writes:
            b = self._buf(t)
            b["w"] = (tk, val)
            b["r"] = {}
        for t in reads:
            self._buf(t)["r"][tk] = val
        return (tk, val)

    def dma(self, eng, out_ap, in_ap, reads=(), writes=()):
        need = self._need(tuple(reads), tuple(writes))
        q = _DMAQ[eng]
        ln = self.lane[q]
        self.lane[q] = (ln + 1) % self.NLANES
        dk = f"{q}{ln}"
        if self.cnt[dk] > 0:
            need[dk] = max(need.get(dk, 0), self.cnt[dk])
        edges = self._edges(eng, need)
        self.cnt[dk] += 16
        val = self.cnt[dk]
        sems = self.sems

        def emit(h, edges=edges, out_ap=out_ap, in_ap=in_ap, dk=dk):
            for k, v in edges:
                h.wait_ge(sems[k], v)
            h.dma_start(out=out_ap, in_=in_ap).then_inc(sems[dk], 16)

        self.recs[eng].append(emit)
        for t in writes:
            b = self._buf(t)
            b["w"] = (dk, val)
            b["r"] = {}
        for t in reads:
            self._buf(t)["r"][dk] = val
        return (dk, val)

    def mm_group(self, items, psum, extra_reads=()):
        psums = psum if isinstance(psum, (list, tuple)) else [psum]
        all_reads = {(t if isinstance(t, str) else t.name) for t in extra_reads}
        for _, rd in items:
            all_reads.update((t if isinstance(t, str) else t.name) for t in rd)
        need = self._need(tuple(all_reads), tuple(psums))
        edges = self._edges("pe", need)
        tk = "tpe"
        self.cnt[tk] += 1
        val = self.cnt[tk]
        sems = self.sems

        def emit(h, edges=edges, items=items, tk=tk):
            for k, v in edges:
                h.wait_ge(sems[k], v)
            last = None
            for build, _ in items:
                last = build(h)
            last.then_inc(sems[tk], 1)

        self.recs["pe"].append(emit)
        for ps_ in psums:
            b = self._buf(ps_)
            b["w"] = (tk, val)
            b["r"] = {}
        for t in all_reads:
            self._buf(t)["r"][tk] = val
        return (tk, val)

    def barrier(self):
        snap = dict(self.cnt)
        sems = self.sems
        for e in ENGS:
            w = self.waited[e]
            todo = [(k, v) for k, v in snap.items() if v > 0 and w[k] < v]
            for k, v in todo:
                w[k] = v

            def emit(h, todo=todo):
                for k, v in todo:
                    h.wait_ge(sems[k], v)
            self.recs[e].append(emit)

    def raw(self, eng, fn):
        self.recs[eng].append(fn)

    def emit_all(self):
        nc = self.nc
        recs = self.recs
        with nc.Block() as block:
            @block.sync
            def _(h):
                for f in recs["sp"]:
                    f(h)

            @block.scalar
            def _(h):
                for f in recs["act"]:
                    f(h)

            @block.vector
            def _(h):
                for f in recs["dve"]:
                    f(h)

            @block.tensor
            def _(h):
                for f in recs["pe"]:
                    f(h)

            @block.gpsimd
            def _(h):
                for f in recs["pool"]:
                    f(h)


def _build_program():
    nc = bass.Bass("TRN2", target_bir_lowering=False)

    def din(name, shape, dt):
        return nc.dram_tensor(name, list(shape), dt, kind="ExternalInput").ap()

    x_ap = din("x", (S, D), F32)
    wq_ap = din("wq", (D, NH * HD), F32)
    wk_ap = din("wk", (D, NKV * HD), F32)
    wv_ap = din("wv", (D, NKV * HD), F32)
    wo_ap = din("wo", (NH * HD, D), F32)
    qs_ap = din("qs", (128, HD), F32)
    ks_ap = din("ks", (128, HD), F32)
    gbt_ap = din("gbt", (128, 64 * 2048), U8)
    ubt_ap = din("ubt", (128, 64 * 2048), U8)
    dbt_ap = din("dbt", (128, 16 * 8192), U8)
    gsx_ap = din("gsx", (128, 64 * 2048), F32)
    usx_ap = din("usx", (128, 64 * 2048), F32)
    dsx_ap = din("dsx", (128, 16 * 8192), F32)
    cos_ap = din("cosb", (128, 16 * 64), BF16)
    sin_ap = din("sinb", (128, 16 * 64), BF16)
    m1_ap = din("m1", (128, 128), F32)
    m2_ap = din("m2", (128, 128), F32)
    id_ap = din("ident", (128, 128), BF16)
    out_ap = nc.dram_tensor("out", [S, D], F32, kind="ExternalOutput").ap()

    q_sc = nc.dram_tensor("q_sc", [S, D], BF16, kind="Internal").ap()
    ao_sc = nc.dram_tensor("ao_sc", [S, D], BF16, kind="Internal").ap()
    s_sc = nc.dram_tensor("s_sc", [S, D], F32, kind="Internal").ap()
    int_sc = nc.dram_tensor("int_sc", [MLP, S], BF16, kind="Internal").ap()

    root = ExitStack()
    with root:
        p = Prog(nc, root)

        def sb(ctx, name, shape, dt, side=None):
            if side:
                return ctx.enter_context(nc.sbuf_tensor(name, shape, dt, side=side))
            return ctx.enter_context(nc.sbuf_tensor(name, shape, dt))

        def ps(ctx, name, shape, dt):
            return ctx.enter_context(nc.psum_tensor(name, shape, dt))

        ctxK = ExitStack()
        ident = sb(ctxK, "ident_s", [128, 128], BF16, side="right")
        m1 = sb(ctxK, "m1_s", [128, 128], F32, side="right")
        m2 = sb(ctxK, "m2_s", [128, 128], F32, side="right")
        cosb = sb(ctxK, "cos_s", [128, 1024], BF16, side="right")
        sinb = sb(ctxK, "sin_s", [128, 1024], BF16, side="right")
        qsr = sb(ctxK, "qs_s", [128, HD], F32, side="right")
        ksr = sb(ctxK, "ks_s", [128, HD], F32, side="right")
        c_eps_ln = sb(ctxK, "c_eps_ln", [128, 1], F32, side="right")
        c_eps_q = sb(ctxK, "c_eps_q", [128, 1], F32, side="right")
        c_eps_k = sb(ctxK, "c_eps_k", [128, 1], F32, side="right")
        p.op("dve", lambda h: h.memset(c_eps_ln[:], 1e-5), writes=[c_eps_ln])
        p.op("dve", lambda h: h.memset(c_eps_q[:], 2.56e-4), writes=[c_eps_q])
        p.op("dve", lambda h: h.memset(c_eps_k[:], 1e-6), writes=[c_eps_k])
        for t, apx in ((ident, id_ap), (m1, m1_ap), (m2, m2_ap), (cosb, cos_ap),
                       (sinb, sin_ap), (qsr, qs_ap), (ksr, ks_ap)):
            p.dma("sp", t[:], apx, writes=[t])

        ctxA = ExitStack()
        xnT = [sb(ctxA, f"xnT{c}", [128, S], BF16) for c in range(16)]

        with ExitStack() as cl:
            xin = [sb(cl, f"ln_x{i}", [128, D], F32) for i in range(2)]
            sqs = [sb(cl, f"ln_sq{i}", [128, D], F32) for i in range(2)]
            xnt = [sb(cl, f"ln_xn{i}", [128, D], BF16) for i in range(2)]
            st1 = sb(cl, "ln_s1", [128, 2], F32)
            st2 = sb(cl, "ln_s2", [128, 2], F32)
            mean = sb(cl, "ln_mean", [128, 1], F32)
            var = sb(cl, "ln_var", [128, 1], F32)
            rstd = sb(cl, "ln_rstd", [128, 1], F32)
            ptr = [ps(cl, f"ln_ptr{i}", [128, 1024], BF16) for i in range(2)]
            for tt in range(16):
                xi = xin[tt % 2]
                sq = sqs[tt % 2]
                xo = xnt[tt % 2]
                p.dma("sp", xi[:], x_ap[128 * tt:128 * (tt + 1), :], writes=[xi])
                p.op("dve", lambda h, xi=xi: h.tensor_reduce(
                    out=st1[:, 0:1], in_=xi[:], axis=mybir.AxisListType.X, op=ALU.add),
                    reads=[xi], writes=[st1])
                p.op("act", lambda h, xi=xi, sq=sq: h.activation(
                    out=sq[:], in_=xi[:], func=AF.Square, accum_out=st2[:, 0:1]),
                    reads=[xi], writes=[sq, st2])
                p.op("dve", lambda h: h.tensor_scalar(
                    out=mean[:], in0=st1[:, 0:1], scalar1=1.0 / D, scalar2=None, op0=ALU.mult),
                    reads=[st1], writes=[mean])
                p.op("act", lambda h: h.activation(
                    out=var[:], in_=mean[:], func=AF.Square, scale=-1.0),
                    reads=[mean], writes=[var])
                p.op("dve", lambda h: h.scalar_tensor_tensor(
                    out=var[:], in0=st2[:, 0:1], scalar=1.0 / D, in1=var[:],
                    op0=ALU.mult, op1=ALU.subtract),
                    reads=[st2, var], writes=[var])
                p.op("act", lambda h: h.activation(
                    out=rstd[:], in_=var[:], func=AF.Sqrt, bias=c_eps_ln[:]),
                    reads=[var, c_eps_ln], writes=[rstd])
                p.op("dve", lambda h: h.reciprocal(out=rstd[:], in_=rstd[:]),
                     reads=[rstd], writes=[rstd])
                p.op("dve", lambda h, xi=xi, xo=xo: h.tensor_scalar(
                    out=xo[:], in0=xi[:], scalar1=mean[:], scalar2=rstd[:],
                    op0=ALU.subtract, op1=ALU.mult),
                    reads=[xi, mean, rstd], writes=[xo])
                for kc in range(16):
                    c = kc if kc < 8 else kc - 8
                    off = 256 * c + (0 if kc < 8 else 1)
                    pt = ptr[kc % 2]
                    p.op("pe", lambda h, xo=xo, off=off, pt=pt: h.transpose(
                        pt[:, 0:128], xo[:, off:off + 255:2], ident[:]),
                        reads=[xo, ident], writes=[pt])
                    p.op("dve", lambda h, pt=pt, kc=kc, tt=tt: h.tensor_copy(
                        out=xnT[kc][:, 128 * tt:128 * (tt + 1)], in_=pt[:, 0:128]),
                        reads=[pt], writes=[xnT[kc]])
            p.barrier()

        with ExitStack() as cq:
            wqs = [sb(cq, f"wq{c}", [128, D], BF16) for c in range(16)]
            for c in range(16):
                p.dma("pool", wqs[c][:], wq_ap[128 * c:128 * (c + 1), :], writes=[wqs[c]])
            psq2 = [ps(cq, f"psq{i}", [128, 2048], F32) for i in range(2)]
            qn = sb(cq, "qn", [128, D], F32)
            qt = [sb(cq, f"qt{i}", [128, D], BF16) for i in range(2)]
            ssq = sb(cq, "q_ssq", [128, 8], F32)
            rst = sb(cq, "q_rst", [128, 8], F32)
            sqsc = sb(cq, "q_sqsc", [128, D], F32)
            r0 = sb(cq, "q_r0", [128, 8 * 64], F32)
            r1 = sb(cq, "q_r1", [128, 8 * 64], F32)
            for tt in range(16):
                psq = psq2[tt % 2]
                items = []
                for kc in range(16):
                    for qb in range(4):
                        def bld(h, kc=kc, qb=qb, tt=tt, psq=psq):
                            return h.matmul(psq[:, 512 * qb:512 * (qb + 1)],
                                            lhsT=xnT[kc][:, 128 * tt:128 * (tt + 1)],
                                            rhs=wqs[kc][:, 512 * qb:512 * (qb + 1)],
                                            start=(kc == 0), stop=(kc == 15))
                        items.append((bld, (xnT[kc], wqs[kc])))
                p.mm_group(items, [f"{psq.name}_b{qb}" for qb in range(4)])
                for hh in range(8):
                    p.op("act", lambda h, hh=hh, psq=psq: h.activation(
                        out=sqsc[:, 256 * hh:256 * (hh + 1)],
                        in_=psq[:, 256 * hh:256 * (hh + 1)], func=AF.Square,
                        accum_out=ssq[:, hh:hh + 1]),
                        reads=[f"{psq.name}_b{hh // 2}"], writes=[sqsc, ssq])
                p.op("act", lambda h: h.activation(
                    out=rst[:], in_=ssq[:], func=AF.Sqrt, bias=c_eps_q[:]),
                    reads=[ssq, c_eps_q], writes=[rst])
                p.op("dve", lambda h: h.reciprocal(out=rst[:], in_=rst[:]),
                     reads=[rst], writes=[rst])
                for hh in range(8):
                    p.op("dve", lambda h, hh=hh, psq=psq: h.scalar_tensor_tensor(
                        out=qn[:, 256 * hh:256 * (hh + 1)],
                        in0=psq[:, 256 * hh:256 * (hh + 1)],
                        scalar=rst[:, hh:hh + 1], in1=qsr[:],
                        op0=ALU.mult, op1=ALU.mult),
                        reads=[f"{psq.name}_b{hh // 2}", rst, qsr], writes=[qn])
                qo = qt[tt % 2]
                c_t = cosb[:, 64 * tt:64 * (tt + 1)]
                s_t = sinb[:, 64 * tt:64 * (tt + 1)]
                x0 = qn[:].rearrange("p (h d) -> p h d", d=256)[:, :, 0:64]
                x1 = qn[:].rearrange("p (h d) -> p h d", d=256)[:, :, 64:128]
                cB = c_t.unsqueeze(1).broadcast_to([128, 8, 64])
                sB = s_t.unsqueeze(1).broadcast_to([128, 8, 64])
                r0v = r0[:].rearrange("p (h d) -> p h d", d=64)
                r1v = r1[:].rearrange("p (h d) -> p h d", d=64)
                qe = qo[:].rearrange("p (h d) -> p h d", d=256)[:, :, 0:64]
                qo_ = qo[:].rearrange("p (h d) -> p h d", d=256)[:, :, 64:128]
                qrest = qo[:].rearrange("p (h d) -> p h d", d=256)[:, :, 128:256]
                xrest = qn[:].rearrange("p (h d) -> p h d", d=256)[:, :, 128:256]
                p.op("dve", lambda h, x0=x0, cB=cB, r0v=r0v: h.tensor_tensor(
                    out=r0v, in0=x0, in1=cB, op=ALU.mult), reads=[qn, cosb], writes=[r0])
                p.op("dve", lambda h, x1=x1, sB=sB, r1v=r1v: h.tensor_tensor(
                    out=r1v, in0=x1, in1=sB, op=ALU.mult), reads=[qn, sinb], writes=[r1])
                p.op("dve", lambda h, r0v=r0v, r1v=r1v, qe=qe: h.tensor_tensor(
                    out=qe, in0=r0v, in1=r1v, op=ALU.subtract),
                    reads=[r0, r1], writes=[qo])
                p.op("dve", lambda h, x0=x0, sB=sB, r0v=r0v: h.tensor_tensor(
                    out=r0v, in0=x0, in1=sB, op=ALU.mult), reads=[qn, sinb], writes=[r0])
                p.op("dve", lambda h, x1=x1, cB=cB, r1v=r1v: h.tensor_tensor(
                    out=r1v, in0=x1, in1=cB, op=ALU.mult), reads=[qn, cosb], writes=[r1])
                p.op("dve", lambda h, r0v=r0v, r1v=r1v, qo_=qo_: h.tensor_tensor(
                    out=qo_, in0=r0v, in1=r1v, op=ALU.add),
                    reads=[r0, r1], writes=[qo])
                p.op("dve", lambda h, xrest=xrest, qrest=qrest: h.tensor_copy(
                    out=qrest, in_=xrest), reads=[qn], writes=[qo])
                p.dma("sp", q_sc[128 * tt:128 * (tt + 1), :], qo[:], reads=[qo],
                      writes=["q_sc"])
            p.barrier()

        ctxB = ExitStack()
        kT = [sb(ctxB, f"kT{c}", [128, S], BF16) for c in range(8)]
        Vt = [sb(ctxB, f"V{t}", [128, NKV * HD], BF16) for t in range(16)]

        with ExitStack() as ck:
            wks = [sb(ck, f"wk{c}", [128, NKV * HD], BF16) for c in range(16)]
            for c in range(16):
                p.dma("pool", wks[c][:], wk_ap[128 * c:128 * (c + 1), :], writes=[wks[c]])
            psk2 = [ps(ck, f"psk{i}", [128, 1024], F32) for i in range(2)]
            kn = sb(ck, "kn", [128, 1024], F32)
            kt_ = sb(ck, "kt_", [128, 1024], BF16)
            kssq = sb(ck, "k_ssq", [128, 4], F32)
            krst = sb(ck, "k_rst", [128, 4], F32)
            ksqsc = sb(ck, "k_sqsc", [128, 1024], F32)
            kr0 = sb(ck, "k_r0", [128, 4 * 64], F32)
            kr1 = sb(ck, "k_r1", [128, 4 * 64], F32)
            ptk = [ps(ck, f"ptk{i}", [128, 1024], BF16) for i in range(2)]
            for tt in range(16):
                psk = psk2[tt % 2]
                items = []
                for kc in range(16):
                    for kb in range(2):
                        def bld(h, kc=kc, kb=kb, tt=tt, psk=psk):
                            return h.matmul(psk[:, 512 * kb:512 * (kb + 1)],
                                            lhsT=xnT[kc][:, 128 * tt:128 * (tt + 1)],
                                            rhs=wks[kc][:, 512 * kb:512 * (kb + 1)],
                                            start=(kc == 0), stop=(kc == 15))
                        items.append((bld, (xnT[kc], wks[kc])))
                p.mm_group(items, [f"{psk.name}_b{kb}" for kb in range(2)])
                for hh in range(4):
                    p.op("act", lambda h, hh=hh, psk=psk: h.activation(
                        out=ksqsc[:, 256 * hh:256 * (hh + 1)],
                        in_=psk[:, 256 * hh:256 * (hh + 1)], func=AF.Square,
                        accum_out=kssq[:, hh:hh + 1]),
                        reads=[f"{psk.name}_b{hh // 2}"], writes=[ksqsc, kssq])
                p.op("act", lambda h: h.activation(
                    out=krst[:], in_=kssq[:], func=AF.Sqrt, scale=1.0 / 256, bias=c_eps_k[:]),
                    reads=[kssq, c_eps_k], writes=[krst])
                p.op("dve", lambda h: h.reciprocal(out=krst[:], in_=krst[:]),
                     reads=[krst], writes=[krst])
                for hh in range(4):
                    p.op("dve", lambda h, hh=hh, psk=psk: h.scalar_tensor_tensor(
                        out=kn[:, 256 * hh:256 * (hh + 1)],
                        in0=psk[:, 256 * hh:256 * (hh + 1)],
                        scalar=krst[:, hh:hh + 1], in1=ksr[:],
                        op0=ALU.mult, op1=ALU.mult),
                        reads=[f"{psk.name}_b{hh // 2}", krst, ksr], writes=[kn])
                c_t = cosb[:, 64 * tt:64 * (tt + 1)]
                s_t = sinb[:, 64 * tt:64 * (tt + 1)]
                x0 = kn[:].rearrange("p (h d) -> p h d", d=256)[:, :, 0:64]
                x1 = kn[:].rearrange("p (h d) -> p h d", d=256)[:, :, 64:128]
                cB = c_t.unsqueeze(1).broadcast_to([128, 4, 64])
                sB = s_t.unsqueeze(1).broadcast_to([128, 4, 64])
                r0v = kr0[:].rearrange("p (h d) -> p h d", d=64)
                r1v = kr1[:].rearrange("p (h d) -> p h d", d=64)
                ke = kt_[:].rearrange("p (h d) -> p h d", d=256)[:, :, 0:64]
                ko = kt_[:].rearrange("p (h d) -> p h d", d=256)[:, :, 64:128]
                krest = kt_[:].rearrange("p (h d) -> p h d", d=256)[:, :, 128:256]
                xrest = kn[:].rearrange("p (h d) -> p h d", d=256)[:, :, 128:256]
                p.op("dve", lambda h, x0=x0, cB=cB, r0v=r0v: h.tensor_tensor(
                    out=r0v, in0=x0, in1=cB, op=ALU.mult), reads=[kn, cosb], writes=[kr0])
                p.op("dve", lambda h, x1=x1, sB=sB, r1v=r1v: h.tensor_tensor(
                    out=r1v, in0=x1, in1=sB, op=ALU.mult), reads=[kn, sinb], writes=[kr1])
                p.op("dve", lambda h, r0v=r0v, r1v=r1v, ke=ke: h.tensor_tensor(
                    out=ke, in0=r0v, in1=r1v, op=ALU.subtract),
                    reads=[kr0, kr1], writes=[kt_])
                p.op("dve", lambda h, x0=x0, sB=sB, r0v=r0v: h.tensor_tensor(
                    out=r0v, in0=x0, in1=sB, op=ALU.mult), reads=[kn, sinb], writes=[kr0])
                p.op("dve", lambda h, x1=x1, cB=cB, r1v=r1v: h.tensor_tensor(
                    out=r1v, in0=x1, in1=cB, op=ALU.mult), reads=[kn, cosb], writes=[kr1])
                p.op("dve", lambda h, r0v=r0v, r1v=r1v, ko=ko: h.tensor_tensor(
                    out=ko, in0=r0v, in1=r1v, op=ALU.add),
                    reads=[kr0, kr1], writes=[kt_])
                p.op("dve", lambda h, xrest=xrest, krest=krest: h.tensor_copy(
                    out=krest, in_=xrest), reads=[kn], writes=[kt_])
                for dc in range(8):
                    pt = ptk[dc % 2]
                    p.op("pe", lambda h, dc=dc, pt=pt: h.transpose(
                        pt[:, 0:128], kt_[:, 128 * dc:128 * (dc + 1)], ident[:]),
                        reads=[kt_, ident], writes=[pt])
                    p.op("dve", lambda h, pt=pt, dc=dc, tt=tt: h.tensor_copy(
                        out=kT[dc][:, 128 * tt:128 * (tt + 1)], in_=pt[:, 0:128]),
                        reads=[pt], writes=[kT[dc]])
            p.barrier()

        with ExitStack() as cv:
            wvs = [sb(cv, f"wv{c}", [128, NKV * HD], BF16) for c in range(16)]
            for c in range(16):
                p.dma("pool", wvs[c][:], wv_ap[128 * c:128 * (c + 1), :], writes=[wvs[c]])
            psv2 = [ps(cv, f"psv{i}", [128, 1024], F32) for i in range(2)]
            for tt in range(16):
                psv = psv2[tt % 2]
                items = []
                for kc in range(16):
                    for kb in range(2):
                        def bld(h, kc=kc, kb=kb, tt=tt, psv=psv):
                            return h.matmul(psv[:, 512 * kb:512 * (kb + 1)],
                                            lhsT=xnT[kc][:, 128 * tt:128 * (tt + 1)],
                                            rhs=wvs[kc][:, 512 * kb:512 * (kb + 1)],
                                            start=(kc == 0), stop=(kc == 15))
                        items.append((bld, (xnT[kc], wvs[kc])))
                p.mm_group(items, [f"{psv.name}_b{kb}" for kb in range(2)])
                p.op("act", lambda h, tt=tt, psv=psv: h.activation(
                    out=Vt[tt][:], in_=psv[:], func=AF.Copy),
                    reads=[f"{psv.name}_b0", f"{psv.name}_b1"], writes=[Vt[tt]])
            p.barrier()

        with ExitStack() as ca:
            qin = [sb(ca, f"at_q{i}", [128, D], BF16) for i in range(2)]
            qTt = sb(ca, "at_qT", [128, D], BF16)
            sS = [sb(ca, f"at_sS{i}", [128, 1152], F32) for i in range(2)]
            pbf = [sb(ca, f"at_p{i}", [128, 1152], BF16) for i in range(2)]
            pTs = [sb(ca, f"at_pT{i}", [128, 1152], BF16) for i in range(2)]
            mx = sb(ca, "at_mx", [128, 2], F32)
            ngm = sb(ca, "at_ngm", [128, 2], F32)
            rsum = sb(ca, "at_rs", [128, 2], F32)
            rcp = sb(ca, "at_rcp", [128, 2], F32)
            aot = [sb(ca, f"at_ao{i}", [128, D], BF16) for i in range(2)]
            psS = [ps(ca, f"at_psS{i}", [128, 1536], F32) for i in range(2)]
            psT = ps(ca, "at_psT", [128, 1024], BF16)
            psO = ps(ca, "at_psO", [128, 512], F32)

            def scores(tt, hh, klo, w):
                hk = hh // 2
                pS = psS[hh % 2]
                items = []
                for dh in range(2):
                    o = 0
                    while o < w:
                        bw = min(512, w - o)
                        def bld(h, hh=hh, dh=dh, o=o, bw=bw, klo=klo, hk=hk,
                                pS=pS):
                            return h.matmul(
                                pS[:, o:o + bw],
                                lhsT=qTt[:, 128 * (2 * hh + dh):128 * (2 * hh + dh + 1)],
                                rhs=kT[2 * hk + dh][:, klo + o:klo + o + bw],
                                start=(dh == 0), stop=(dh == 1))
                        items.append((bld, (qTt, kT[2 * hk + dh])))
                        o += bw
                p.mm_group(items, [f"{pS.name}_{i}" for i in range((w + 511) // 512)])

            def softmax_pt_pv(tt, hh, klo, w, ao):
                par = hh % 2
                hk = hh // 2
                pS = psS[par]
                sSx = sS[par]
                pb = pbf[par]
                pT = pTs[par]
                nkt = w // 128
                nblk = (w + 511) // 512
                mxs = mx[:, par:par + 1]
                ngs = ngm[:, par:par + 1]
                rss = rsum[:, par:par + 1]
                rcs = rcp[:, par:par + 1]
                p.op("act", lambda h, w=w, pS=pS, sSx=sSx: h.activation(
                    out=sSx[:, 0:w], in_=pS[:, 0:w], func=AF.Tanh, scale=0.02),
                    reads=[f"{pS.name}_{i}" for i in range(nblk)], writes=[sSx])
                p.op("dve", lambda h, w=w, sSx=sSx: h.tensor_tensor(
                    out=sSx[:, w - 128:w], in0=sSx[:, w - 128:w], in1=m1[:],
                    op=ALU.add), reads=[sSx, m1], writes=[sSx])
                if nkt == 9:
                    p.op("dve", lambda h, sSx=sSx: h.tensor_tensor(
                        out=sSx[:, 0:128], in0=sSx[:, 0:128], in1=m2[:],
                        op=ALU.add), reads=[sSx, m2], writes=[sSx])
                p.op("dve", lambda h, w=w, sSx=sSx, mxs=mxs: h.tensor_reduce(
                    out=mxs, in_=sSx[:, 0:w], axis=mybir.AxisListType.X, op=ALU.max),
                    reads=[sSx], writes=["at_mx"])
                p.op("dve", lambda h, mxs=mxs, ngs=ngs: h.tensor_scalar(
                    out=ngs, in0=mxs, scalar1=-50.0, scalar2=None, op0=ALU.mult),
                    reads=["at_mx"], writes=["at_ngm"])
                p.op("act", lambda h, w=w, sSx=sSx, pb=pb, ngs=ngs, rss=rss:
                     h.activation(out=pb[:, 0:w], in_=sSx[:, 0:w], func=AF.Exp,
                                  scale=50.0, bias=ngs, accum_out=rss),
                     reads=[sSx, "at_ngm"], writes=[pb, "at_rs"])
                p.op("dve", lambda h, rss=rss, rcs=rcs: h.reciprocal(
                    out=rcs, in_=rss), reads=["at_rs"], writes=["at_rcp"])
                n1 = min(nkt, 8)
                for kb in range(n1):
                    p.op("pe", lambda h, kb=kb, pb=pb: h.transpose(
                        psT[:, 128 * kb:128 * (kb + 1)], pb[:, 128 * kb:128 * (kb + 1)],
                        ident[:]), reads=[pb, ident], writes=["at_psT"])
                p.op("act", lambda h, n1=n1, pT=pT: h.activation(
                    out=pT[:, 0:128 * n1], in_=psT[:, 0:128 * n1], func=AF.Copy),
                    reads=["at_psT"], writes=[pT])
                if nkt == 9:
                    p.op("pe", lambda h, pb=pb: h.transpose(
                        psT[:, 0:128], pb[:, 1024:1152], ident[:]),
                        reads=[pb, ident], writes=["at_psT"])
                    p.op("act", lambda h, pT=pT: h.activation(
                        out=pT[:, 1024:1152], in_=psT[:, 0:128], func=AF.Copy),
                        reads=["at_psT"], writes=[pT])
                items = []
                for kb in range(nkt):
                    kt_i = klo // 128 + kb
                    def bld(h, kb=kb, kt_i=kt_i, hk=hk, nkt=nkt, pT=pT):
                        return h.matmul(
                            psO[:, 0:256], lhsT=pT[:, 128 * kb:128 * (kb + 1)],
                            rhs=Vt[kt_i][:, 256 * hk:256 * (hk + 1)],
                            start=(kb == 0), stop=(kb == nkt - 1))
                    items.append((bld, (pT, Vt[kt_i])))
                p.mm_group(items, "at_psO")
                p.op("dve", lambda h, hh=hh, ao=ao, rcs=rcs: h.tensor_scalar(
                    out=ao[:, 256 * hh:256 * (hh + 1)], in0=psO[:, 0:256],
                    scalar1=rcs, scalar2=None, op0=ALU.mult),
                    reads=["at_psO", "at_rcp"], writes=[ao])

            for tt in range(16):
                qi = qin[tt % 2]
                ao = aot[tt % 2]
                p.dma("sp", qi[:], q_sc[128 * tt:128 * (tt + 1), :], reads=["q_sc"],
                      writes=[qi])
                for rnd in range(2):
                    for dc8 in range(8):
                        dc = 8 * rnd + dc8
                        p.op("pe", lambda h, qi=qi, dc=dc, dc8=dc8: h.transpose(
                            psT[:, 128 * dc8:128 * (dc8 + 1)],
                            qi[:, 128 * dc:128 * (dc + 1)], ident[:]),
                            reads=[qi, ident], writes=["at_psT"])
                    p.op("act", lambda h, rnd=rnd: h.activation(
                        out=qTt[:, 1024 * rnd:1024 * (rnd + 1)], in_=psT[:],
                        func=AF.Copy), reads=["at_psT"], writes=[qTt])
                klo = max(0, 128 * (tt - 8))
                w = 128 * (tt + 1) - klo
                scores(tt, 0, klo, w)
                for hh in range(8):
                    if hh + 1 < 8:
                        scores(tt, hh + 1, klo, w)
                    softmax_pt_pv(tt, hh, klo, w, ao)
                p.dma("sp", ao_sc[128 * tt:128 * (tt + 1), :], ao[:], reads=[ao],
                      writes=["ao_sc"])
            p.barrier()
        ctxB.close()
        p.barrier()

        with ExitStack() as cs:
            wos = [sb(cs, f"wo{c}", [128, D], BF16) for c in range(16)]
            for c in range(16):
                p.dma("pool", wos[c][:], wo_ap[128 * c:128 * (c + 1), :], writes=[wos[c]])
            aoin = [sb(cs, f"sp_ao{i}", [128, D], BF16) for i in range(2)]
            aoT = sb(cs, "sp_aoT", [128, D], BF16)
            sout = [sb(cs, f"sp_s{i}", [128, D], F32) for i in range(2)]
            pss2 = [ps(cs, f"sp_pss{i}", [128, 2048], F32) for i in range(2)]
            p.dma("sp", aoin[0][:], ao_sc[0:128, :], reads=["ao_sc"], writes=[aoin[0]])
            for tt in range(16):
                ai = aoin[tt % 2]
                so = sout[tt % 2]
                pss = pss2[tt % 2]
                if tt + 1 < 16:
                    p.dma("sp", aoin[(tt + 1) % 2][:],
                          ao_sc[128 * (tt + 1):128 * (tt + 2), :],
                          reads=["ao_sc"], writes=[aoin[(tt + 1) % 2]])
                pav = pss[:, 0:512].bitcast(BF16)
                tname = f"{pss.name}_T"
                for rnd in range(2):
                    for dc8 in range(8):
                        dc = 8 * rnd + dc8
                        p.op("pe", lambda h, ai=ai, dc=dc, dc8=dc8, pav=pav:
                             h.transpose(pav[:, 128 * dc8:128 * (dc8 + 1)],
                                         ai[:, 128 * dc:128 * (dc + 1)], ident[:]),
                             reads=[ai, ident], writes=[tname])
                    p.op("act", lambda h, rnd=rnd, pav=pav: h.activation(
                        out=aoT[:, 1024 * rnd:1024 * (rnd + 1)], in_=pav[:],
                        func=AF.Copy), reads=[tname], writes=[aoT])
                items = []
                for dc in range(16):
                    for hb in range(4):
                        def bld(h, dc=dc, hb=hb, pss=pss):
                            return h.matmul(pss[:, 512 * hb:512 * (hb + 1)],
                                            lhsT=aoT[:, 128 * dc:128 * (dc + 1)],
                                            rhs=wos[dc][:, 512 * hb:512 * (hb + 1)],
                                            start=(dc == 0), stop=(dc == 15))
                        items.append((bld, (aoT, wos[dc])))
                p.mm_group(items, [f"{pss.name}_{hb}" for hb in range(4)] + [tname])
                p.op("act", lambda h, so=so, pss=pss: h.activation(
                    out=so[:], in_=pss[:], func=AF.Copy),
                    reads=[f"{pss.name}_{hb}" for hb in range(4)], writes=[so])
                p.dma("sp", s_sc[128 * tt:128 * (tt + 1), :], so[:], reads=[so],
                      writes=["s_sc"])
            p.barrier()
        ctxK.close()
        p.barrier()

        NBW = 256
        NGRP = MLP // NBW
        with ExitStack() as cg:
            pk = {m: [sb(cg, f"gu_pk_{m}{i}", [128, 8 * NBW], U8) for i in range(2)]
                  for m in ("g", "u")}
            scx = {m: [sb(cg, f"gu_sc_{m}{i}", [128, 8 * NBW], BF16) for i in range(2)]
                   for m in ("g", "u")}
            cch = {m: [sb(cg, f"gu_C{m}{b}", [128, 16 * NBW], BF16)
                       for b in range(3)] for m in ("g", "u")}
            t7c = sb(cg, "gu_t7", [128, 16 * NBW], U8)
            s8c = sb(cg, "gu_s8", [128, 16 * NBW], U8)
            mgc = sb(cg, "gu_mg", [128, 16 * NBW], BF16)
            sgc = sb(cg, "gu_sg", [128, 16 * NBW], BF16)
            gbuf = sb(cg, "gu_gb", [128, 2048], F32)
            ubuf = sb(cg, "gu_ub", [128, 2048], F32)
            gsq = sb(cg, "gu_gsq", [128, 2048], F32)
            zb = sb(cg, "gu_z", [128, 2048], F32)
            itt = [sb(cg, f"gu_it{i}", [128, 2048], BF16) for i in range(2)]
            pgu = [ps(cg, f"gu_ps{i}", [128, 512], F32) for i in range(8)]

            def decode(mat, g):
                pkt = pk[mat][g % 2]
                sct = scx[mat][g % 2]
                cc = cch[mat][g % 3]
                p16 = pkt[:].bitcast(I16)
                tlo = t7c[:, 0:2048].bitcast(I16)
                thi = t7c[:, 2048:4096].bitcast(I16)
                slo = s8c[:, 0:2048].bitcast(I16)
                shi = s8c[:, 2048:4096].bitcast(I16)
                p.op("dve", lambda h, p16=p16, tlo=tlo: h.tensor_scalar(
                    out=tlo, in0=p16, scalar1=0x0707, scalar2=None,
                    op0=ALU.bitwise_and), reads=[pkt], writes=[t7c])
                p.op("dve", lambda h, p16=p16, thi=thi: h.tensor_scalar(
                    out=thi, in0=p16, scalar1=4, scalar2=0x0707,
                    op0=ALU.logical_shift_right, op1=ALU.bitwise_and),
                    reads=[pkt], writes=[t7c])
                p.op("dve", lambda h, p16=p16, slo=slo: h.tensor_scalar(
                    out=slo, in0=p16, scalar1=0x0808, scalar2=None,
                    op0=ALU.bitwise_and), reads=[pkt], writes=[s8c])
                p.op("dve", lambda h, p16=p16, shi=shi: h.tensor_scalar(
                    out=shi, in0=p16, scalar1=4, scalar2=0x0808,
                    op0=ALU.logical_shift_right, op1=ALU.bitwise_and),
                    reads=[pkt], writes=[s8c])
                bb16 = mgc[:].bitcast(I16)
                p.op("act", lambda h, bb16=bb16: h.activation(
                    out=bb16, in_=t7c[:], func=AF.Copy, scale=64.0, bias=16128.0),
                    reads=[t7c], writes=[mgc])
                p.op("act", lambda h: h.activation(
                    out=sgc[:], in_=s8c[:], func=AF.Copy, scale=-0.25, bias=1.0),
                    reads=[s8c], writes=[sgc])
                p.op("dve", lambda h: h.scalar_tensor_tensor(
                    out=mgc[:], in0=mgc[:], scalar=1.0, in1=mgc[:],
                    op0=ALU.min, op1=ALU.add), reads=[mgc], writes=[mgc])
                p.op("dve", lambda h: h.tensor_scalar(
                    out=mgc[:], in0=mgc[:], scalar1=-1.0, scalar2=None,
                    op0=ALU.add), reads=[mgc], writes=[mgc])
                p.op("dve", lambda h: h.tensor_tensor(
                    out=mgc[:], in0=mgc[:], in1=sgc[:], op=ALU.mult),
                    reads=[mgc, sgc], writes=[mgc])
                scb = sct[:].unsqueeze(1).broadcast_to([128, 2, 2048])
                p.op("dve", lambda h, cc=cc, scb=scb: h.tensor_tensor(
                    out=cc[:].rearrange("p (pl n) -> p pl n", pl=2),
                    in0=mgc[:].rearrange("p (pl n) -> p pl n", pl=2),
                    in1=scb, op=ALU.mult),
                    reads=[mgc, sct], writes=[cc])

            def load_group(mat, ap_w, ap_s, g):
                pkt = pk[mat][g % 2]
                sct = scx[mat][g % 2]
                p.dma("sp", pkt[:], ap_w[:, 2048 * g:2048 * (g + 1)],
                      writes=[pkt])
                p.dma("pool", sct[:], ap_s[:, 2048 * g:2048 * (g + 1)],
                      writes=[sct])

            MATS = (("g", gbt_ap, gsx_ap), ("u", ubt_ap, usx_ap))
            for mat, ap_w, ap_s in MATS:
                load_group(mat, ap_w, ap_s, 0)
                decode(mat, 0)
            MATS2 = (("g", gbuf), ("u", ubuf))
            for gpre in (0, 1):
                for mat, ap_w, ap_s in MATS:
                    load_group(mat, ap_w, ap_s, gpre)
                    decode(mat, gpre)
            for g in range(NGRP):
                buf = g % 3
                if g + 2 < NGRP:
                    for mat, ap_w, ap_s in MATS:
                        load_group(mat, ap_w, ap_s, g + 2)
                        decode(mat, g + 2)
                for nt in range(2):
                    cglob = 2 * g + nt
                    io = itt[nt]
                    for mat, obuf in MATS2:
                        cc = cch[mat][buf]
                        boff = 4 * (0 if mat == "g" else 1)
                        items = []
                        for kc in range(16):
                            pl = 0 if kc < 8 else 1
                            jc = kc % 8
                            for tb in range(4):
                                bank = pgu[boff + tb]
                                def bld(h, cc=cc, jc=jc, pl=pl, nt=nt, tb=tb,
                                        bank=bank, kc=kc):
                                    o = 2048 * pl + NBW * jc + 128 * nt
                                    return h.matmul(
                                        bank[:], lhsT=cc[:, o:o + 128],
                                        rhs=xnT[kc][:, 512 * tb:512 * (tb + 1)],
                                        start=(kc == 0), stop=(kc == 15))
                                items.append((bld, (cc, xnT[kc])))
                        p.mm_group(items, [pgu[boff + tb].name for tb in range(4)])
                        for tb in range(4):
                            bank = pgu[boff + tb]
                            p.op("act", lambda h, bank=bank, tb=tb, obuf=obuf:
                                 h.activation(out=obuf[:, 512 * tb:512 * (tb + 1)],
                                              in_=bank[:], func=AF.Copy),
                                 reads=[bank.name], writes=[obuf])
                    p.op("act", lambda h: h.activation(
                        out=gsq[:], in_=gbuf[:], func=AF.Square),
                        reads=[gbuf], writes=[gsq])
                    p.op("dve", lambda h: h.tensor_scalar(
                        out=gsq[:], in0=gsq[:], scalar1=GB, scalar2=GA,
                        op0=ALU.mult, op1=ALU.add), reads=[gsq], writes=[gsq])
                    p.op("dve", lambda h: h.tensor_tensor(
                        out=zb[:], in0=gsq[:], in1=gbuf[:], op=ALU.mult),
                        reads=[gsq, gbuf], writes=[zb])
                    p.op("act", lambda h: h.activation(
                        out=gsq[:], in_=zb[:], func=AF.Sigmoid),
                        reads=[zb], writes=[gsq])
                    p.op("dve", lambda h: h.tensor_tensor(
                        out=zb[:], in0=gbuf[:], in1=gsq[:], op=ALU.mult),
                        reads=[gbuf, gsq], writes=[zb])
                    p.op("dve", lambda h, io=io: h.tensor_tensor(
                        out=io[:], in0=zb[:], in1=ubuf[:], op=ALU.mult),
                        reads=[zb, ubuf], writes=[io])
                    p.dma("sp", int_sc[128 * cglob:128 * (cglob + 1), :], io[:],
                          reads=[io], writes=["int_sc"])
            p.barrier()
        ctxA.close()
        p.barrier()

        with ExitStack() as cd:
            pkd = [sb(cd, f"dn_pk{i}", [128, 4096], U8) for i in range(2)]
            scd = [sb(cd, f"dn_sc{i}", [128, 4096], BF16) for i in range(2)]
            cdh = [sb(cd, f"dn_C{x}", [128, 4096], BF16) for x in range(3)]
            itr = [sb(cd, f"dn_it{i}", [128, 2048], BF16) for i in range(8)]
            dt7 = sb(cd, "dn_t7", [128, 4096], U8)
            ds8 = sb(cd, "dn_s8", [128, 4096], U8)
            dbb = sb(cd, "dn_bb", [128, 4096], I16)
            dmg = sb(cd, "dn_mg", [128, 4096], BF16)
            dsg = sb(cd, "dn_sg", [128, 4096], BF16)
            acc = [sb(cd, f"acc{t}", [128, 1024], F32) for t in range(16)]
            xf = [sb(cd, f"fn_x{i}", [128, 1024], F32) for i in range(2)]
            sf = [sb(cd, f"fn_s{i}", [128, 1024], F32) for i in range(2)]
            gf = [sb(cd, f"fn_g{i}", [128, 1024], F32) for i in range(2)]
            pdn = [ps(cd, f"dn_ps{i}", [128, 1024], F32) for i in range(4)]

            def dn_load(ss, half):
                pp = pkd[ss % 2]
                sc = scd[ss % 2]
                base = ((ss % 16) * 2 + half) * 4096
                p.dma("pool", pp[:], dbt_ap[:, base:base + 4096], writes=[pp])
                p.dma("pool", sc[:],
                      dsx_ap[:, 8192 * (ss % 16) + 4096 * half:
                             8192 * (ss % 16) + 4096 * (half + 1)],
                      writes=[sc])

            def dn_decode(ss, slot):
                lo_plane = ss < 16
                pp = pkd[ss % 2]
                sc = scd[ss % 2]
                dst = cdh[slot]
                s16 = pp[:].bitcast(I16)
                t16 = dt7[:].bitcast(I16)
                x16 = ds8[:].bitcast(I16)
                if lo_plane:
                    p.op("dve", lambda h, s16=s16, t16=t16: h.tensor_scalar(
                        out=t16, in0=s16, scalar1=0x0707, scalar2=None,
                        op0=ALU.bitwise_and), reads=[pp], writes=[dt7])
                    p.op("dve", lambda h, s16=s16, x16=x16: h.tensor_scalar(
                        out=x16, in0=s16, scalar1=0x0808, scalar2=None,
                        op0=ALU.bitwise_and), reads=[pp], writes=[ds8])
                else:
                    p.op("dve", lambda h, s16=s16, t16=t16: h.tensor_scalar(
                        out=t16, in0=s16, scalar1=4, scalar2=0x0707,
                        op0=ALU.logical_shift_right, op1=ALU.bitwise_and),
                        reads=[pp], writes=[dt7])
                    p.op("dve", lambda h, s16=s16, x16=x16: h.tensor_scalar(
                        out=x16, in0=s16, scalar1=4, scalar2=0x0808,
                        op0=ALU.logical_shift_right, op1=ALU.bitwise_and),
                        reads=[pp], writes=[ds8])
                p.op("act", lambda h: h.activation(
                    out=dbb[:], in_=dt7[:], func=AF.Copy, scale=64.0, bias=16128.0),
                    reads=[dt7], writes=[dbb])
                p.op("act", lambda h: h.activation(
                    out=dsg[:], in_=ds8[:], func=AF.Copy, scale=-0.25, bias=1.0),
                    reads=[ds8], writes=[dsg])
                ab = dbb[:].bitcast(BF16)
                p.op("dve", lambda h, ab=ab: h.tensor_scalar(
                    out=dmg[:], in0=ab, scalar1=1.0, scalar2=None,
                    op0=ALU.min), reads=[dbb], writes=[dmg])
                p.op("dve", lambda h, ab=ab: h.tensor_tensor(
                    out=dmg[:], in0=dmg[:], in1=ab, op=ALU.add),
                    reads=[dmg, dbb], writes=[dmg])
                p.op("dve", lambda h: h.tensor_scalar(
                    out=dmg[:], in0=dmg[:], scalar1=-1.0, scalar2=None,
                    op0=ALU.add), reads=[dmg], writes=[dmg])
                p.op("dve", lambda h: h.tensor_tensor(
                    out=dmg[:], in0=dmg[:], in1=dsg[:], op=ALU.mult),
                    reads=[dmg, dsg], writes=[dmg])
                p.op("pool", lambda h, dst=dst, sc=sc: h.tensor_tensor(
                    out=dst[:], in0=dmg[:], in1=sc[:], op=ALU.mult),
                    reads=[dmg, sc], writes=[dst])

            def dn_iload(ss):
                for ns in range(4):
                    cglob = 4 * ss + ns
                    slot = itr[cglob % 8]
                    p.dma("sp", slot[:], int_sc[128 * cglob:128 * (cglob + 1), :],
                          reads=["int_sc"], writes=[slot])

            def dn_mm(ss, slot):
                cx = cdh[slot]
                for tt in range(16):
                    bank = pdn[tt % 4]
                    items = []
                    for ns in range(4):
                        cglob = 4 * ss + ns
                        sl = itr[cglob % 8]
                        for hb in range(2):
                            def bld(h, tt=tt, ns=ns, hb=hb, bank=bank, sl=sl,
                                    cx=cx):
                                return h.matmul(
                                    bank[:, 512 * hb:512 * (hb + 1)],
                                    lhsT=sl[:, 128 * tt:128 * (tt + 1)],
                                    rhs=cx[:, 1024 * ns + 512 * hb:
                                           1024 * ns + 512 * (hb + 1)],
                                    start=(ns == 0), stop=(ns == 3))
                            items.append((bld, (sl, cx)))
                    p.mm_group(items, bank.name)
                    if ss == 0:
                        p.op("dve", lambda h, bank=bank, tt=tt: h.tensor_copy(
                            out=acc[tt][:], in_=bank[:]),
                            reads=[bank.name], writes=[acc[tt]])
                    else:
                        p.op("dve", lambda h, bank=bank, tt=tt: h.tensor_tensor(
                            out=acc[tt][:], in0=acc[tt][:], in1=bank[:],
                            op=ALU.add),
                            reads=[bank.name, acc[tt]], writes=[acc[tt]])

            def finalize(half):
                for tt in range(16):
                    xi = xf[tt % 2]
                    si = sf[tt % 2]
                    gi = gf[tt % 2]
                    p.dma("act", xi[:],
                          x_ap[128 * tt:128 * (tt + 1),
                               1024 * half:1024 * (half + 1)], writes=[xi])
                    p.dma("act", si[:],
                          s_sc[128 * tt:128 * (tt + 1),
                               1024 * half:1024 * (half + 1)],
                          reads=["s_sc"], writes=[si])
                    p.op("act", lambda h, tt=tt, gi=gi: h.activation(
                        out=gi[:], in_=acc[tt][:], func=AF.Silu),
                        reads=[acc[tt]], writes=[gi])
                    p.op("dve", lambda h, gi=gi, si=si: h.tensor_tensor(
                        out=gi[:], in0=gi[:], in1=si[:], op=ALU.mult),
                        reads=[gi, si], writes=[gi])
                    p.op("dve", lambda h, gi=gi, xi=xi: h.tensor_tensor(
                        out=gi[:], in0=gi[:], in1=xi[:], op=ALU.add),
                        reads=[gi, xi], writes=[gi])
                    p.dma("act",
                          out_ap[128 * tt:128 * (tt + 1),
                                 1024 * half:1024 * (half + 1)], gi[:],
                          reads=[gi], writes=["out_hbm"])

            for half in range(2):
                dn_iload(0)
                dn_load(0, half)
                dn_decode(0, (32 * half) % 3)
                if half == 1:
                    finalize(0)
                for ss in range(32):
                    step = 32 * half + ss
                    if ss + 1 < 32:
                        dn_load(ss + 1, half)
                        dn_decode(ss + 1, (step + 1) % 3)
                        dn_iload(ss + 1)
                    dn_mm(ss, step % 3)
            finalize(1)
            p.barrier()

        p.emit_all()
    return nc


def _host_layout(inputs):
    x = np.asarray(inputs["L0_attn_x"])
    Wq = np.asarray(inputs["L0_attn_Wq"])
    Wk = np.asarray(inputs["L0_attn_Wk"])
    Wv = np.asarray(inputs["L0_attn_Wv"])
    Wo = np.asarray(inputs["L0_attn_Wo"])
    q_scale = np.asarray(inputs["L0_attn_q_scale"])
    k_scale = np.asarray(inputs["L0_attn_k_scale"])
    gW = np.asarray(inputs["L0_ffn_gate_W_packed"])
    gS = np.asarray(inputs["L0_ffn_gate_scales"])
    uW = np.asarray(inputs["L0_ffn_up_W_packed"])
    uS = np.asarray(inputs["L0_ffn_up_scales"])
    dW = np.asarray(inputs["L0_ffn_down_W_packed"])
    dS = np.asarray(inputs["L0_ffn_down_scales"])

    rho = np.concatenate([np.arange(0, D, 2), np.arange(1, D, 2)])
    pi = np.concatenate([np.arange(0, ROT, 2), np.arange(1, ROT, 2),
                         np.arange(ROT, HD)])
    piq = np.concatenate([pi + HD * h for h in range(NH)])
    pik = np.concatenate([pi + HD * h for h in range(NKV)])
    sig = np.concatenate([np.arange(0, MLP, 2), np.arange(1, MLP, 2)])

    shared = {}
    shared["wq"] = np.ascontiguousarray(Wq[rho][:, piq], dtype=np.float32)
    shared["wk"] = np.ascontiguousarray(Wk[rho][:, pik], dtype=np.float32)
    shared["wv"] = np.ascontiguousarray(Wv[rho], dtype=np.float32)
    shared["wo"] = np.ascontiguousarray(Wo, dtype=np.float32)
    shared["qs"] = np.ascontiguousarray(
        np.broadcast_to(q_scale[pi].astype(np.float32), (128, HD)))
    shared["ks"] = np.ascontiguousarray(
        np.broadcast_to(k_scale[pi].astype(np.float32), (128, HD)))

    def bytes_of(Wp):
        return Wp.view(np.uint8).reshape(Wp.shape[0], Wp.shape[1], 4)[:, :, 0]

    def gu_pack(Wp):
        B = bytes_of(Wp)[sig].T
        return np.ascontiguousarray(
            B.reshape(8, 128, 64, 256).transpose(1, 2, 0, 3).reshape(128, 64 * 2048))
    shared["gbt"] = gu_pack(gW)
    shared["ubt"] = gu_pack(uW)
    Dt = bytes_of(dW).T
    shared["dbt"] = np.ascontiguousarray(
        Dt.reshape(16, 4, 128, 2, 1024).transpose(2, 0, 3, 1, 4).reshape(128, 16 * 8192))
    p16 = np.arange(128) // 16
    def gu_exp(Ssc):
        gsT = Ssc[sig].T.astype(np.float32)
        rows = 8 * np.arange(8)[None, :] + p16[:, None]
        B1 = gsT[rows]
        return np.ascontiguousarray(
            B1.reshape(128, 8, 64, 256).transpose(0, 2, 1, 3).reshape(128, 64 * 2048))
    shared["gsx"] = gu_exp(gS)
    shared["usx"] = gu_exp(uS)
    dsT = dS.T.astype(np.float32)
    rows = (32 * np.arange(16)[None, :, None] + 8 * np.arange(4)[None, None, :]
            + p16[:, None, None])
    D1 = dsT[rows]
    D1 = D1.reshape(128, 16, 4, 2, 1024).transpose(0, 1, 3, 2, 4)
    shared["dsx"] = np.ascontiguousarray(D1.reshape(128, 16 * 8192))

    theta = 1.0 / (10000.0 ** (np.arange(0, ROT, 2, dtype=np.float32) / ROT))
    pos = np.arange(S, dtype=np.float32)
    ang = pos[:, None] * theta[None, :]
    cosc = np.cos(ang).astype(np.float32).reshape(16, 128, 64).transpose(1, 0, 2)
    sinc = np.sin(ang).astype(np.float32).reshape(16, 128, 64).transpose(1, 0, 2)
    shared["cosb"] = np.ascontiguousarray(cosc.reshape(128, 1024)).astype(BF)
    shared["sinb"] = np.ascontiguousarray(sinc.reshape(128, 1024)).astype(BF)
    ii = np.arange(128)[:, None]
    jj = np.arange(128)[None, :]
    shared["m1"] = np.where(jj <= ii, 0.0, NEG50).astype(np.float32)
    shared["m2"] = np.where(jj > ii, 0.0, NEG50).astype(np.float32)
    shared["ident"] = np.eye(128, dtype=BF)

    in_maps = []
    for c in range(8):
        m = dict(shared)
        m["x"] = np.ascontiguousarray(x[c], dtype=np.float32)
        in_maps.append(m)
    return in_maps


def kernel(**inputs):
    in_maps = _host_layout(inputs)
    nc = _build_program()
    res = run_bass_kernel_spmd(nc, in_maps, core_ids=list(range(8)))
    out = np.stack([np.asarray(r["out"]) for r in res.results])
    return out.astype(np.float32)


# version 14
# speedup vs baseline: 1.1764x; 1.0943x over previous
import numpy as np
import ml_dtypes
from contextlib import ExitStack

import concourse.bass as bass
import concourse.mybir as mybir
from concourse.bass_utils import run_bass_kernel_spmd

F32 = mybir.dt.float32
BF16 = mybir.dt.bfloat16
U8 = mybir.dt.uint8
I16 = mybir.dt.int16
AF = mybir.ActivationFunctionType
ALU = mybir.AluOpType
BF = ml_dtypes.bfloat16

S = 2048
D = 2048
NH, NKV, HD = 8, 4, 256
ROT = 128
WIN = 1024
NEG50 = -2.3819763e38 / 50.0
GA = 2.0 * 0.7978845608028654
GB = GA * 0.044715
MLP = 16384

ENGS = ("sp", "act", "dve", "pe", "pool")
_TICK = {"pe": "tpe", "dve": "tdve", "act": "tact", "pool": "tpool", "sp": "tsp"}
_DMAQ = {"sp": "dsp", "act": "dact", "pool": "dpool"}


class Prog:

    NLANES = 8

    def __init__(self, nc, ctx):
        self.nc = nc
        self.recs = {e: [] for e in ENGS}
        self.sems = {}
        names = ["tpe", "tdve", "tact", "tpool", "tsp"]
        names += [f"{q}{i}" for q in ("dsp", "dact", "dpool") for i in range(self.NLANES)]
        for name in names:
            self.sems[name] = ctx.enter_context(nc.semaphore(name))
        self.cnt = {k: 0 for k in self.sems}
        self.lane = {"dsp": 0, "dact": 0, "dpool": 0}
        self.waited = {e: {k: 0 for k in self.sems} for e in ENGS}
        self.bufs = {}

    def _buf(self, t):
        n = t if isinstance(t, str) else t.name
        return self.bufs.setdefault(n, {"w": None, "r": {}})

    def _need(self, reads, war):
        need = {}
        for t in reads:
            b = self._buf(t)
            if b["w"]:
                k, v = b["w"]
                need[k] = max(need.get(k, 0), v)
        for t in war:
            b = self._buf(t)
            if b["w"]:
                k, v = b["w"]
                need[k] = max(need.get(k, 0), v)
            for k, v in b["r"].items():
                need[k] = max(need.get(k, 0), v)
        return need

    def _edges(self, eng, need):
        out = []
        w = self.waited[eng]
        for k, v in need.items():
            if w[k] < v:
                out.append((k, v))
                w[k] = v
        return out

    def op(self, eng, build, reads=(), writes=(), war=()):
        need = self._need(tuple(reads), tuple(writes) + tuple(war))
        edges = self._edges(eng, need)
        tk = _TICK[eng]
        self.cnt[tk] += 1
        val = self.cnt[tk]
        sems = self.sems

        def emit(h, edges=edges, build=build, tk=tk):
            for k, v in edges:
                h.wait_ge(sems[k], v)
            build(h).then_inc(sems[tk], 1)

        self.recs[eng].append(emit)
        for t in writes:
            b = self._buf(t)
            b["w"] = (tk, val)
            b["r"] = {}
        for t in reads:
            self._buf(t)["r"][tk] = val
        return (tk, val)

    def dma(self, eng, out_ap, in_ap, reads=(), writes=()):
        need = self._need(tuple(reads), tuple(writes))
        q = _DMAQ[eng]
        ln = self.lane[q]
        self.lane[q] = (ln + 1) % self.NLANES
        dk = f"{q}{ln}"
        if self.cnt[dk] > 0:
            need[dk] = max(need.get(dk, 0), self.cnt[dk])
        edges = self._edges(eng, need)
        self.cnt[dk] += 16
        val = self.cnt[dk]
        sems = self.sems

        def emit(h, edges=edges, out_ap=out_ap, in_ap=in_ap, dk=dk):
            for k, v in edges:
                h.wait_ge(sems[k], v)
            h.dma_start(out=out_ap, in_=in_ap).then_inc(sems[dk], 16)

        self.recs[eng].append(emit)
        for t in writes:
            b = self._buf(t)
            b["w"] = (dk, val)
            b["r"] = {}
        for t in reads:
            self._buf(t)["r"][dk] = val
        return (dk, val)

    def mm_group(self, items, psum, extra_reads=()):
        psums = psum if isinstance(psum, (list, tuple)) else [psum]
        all_reads = {(t if isinstance(t, str) else t.name) for t in extra_reads}
        for _, rd in items:
            all_reads.update((t if isinstance(t, str) else t.name) for t in rd)
        need = self._need(tuple(all_reads), tuple(psums))
        edges = self._edges("pe", need)
        tk = "tpe"
        self.cnt[tk] += 1
        val = self.cnt[tk]
        sems = self.sems

        def emit(h, edges=edges, items=items, tk=tk):
            for k, v in edges:
                h.wait_ge(sems[k], v)
            last = None
            for build, _ in items:
                last = build(h)
            last.then_inc(sems[tk], 1)

        self.recs["pe"].append(emit)
        for ps_ in psums:
            b = self._buf(ps_)
            b["w"] = (tk, val)
            b["r"] = {}
        for t in all_reads:
            self._buf(t)["r"][tk] = val
        return (tk, val)

    def barrier(self):
        snap = dict(self.cnt)
        sems = self.sems
        for e in ENGS:
            w = self.waited[e]
            todo = [(k, v) for k, v in snap.items() if v > 0 and w[k] < v]
            for k, v in todo:
                w[k] = v

            def emit(h, todo=todo):
                for k, v in todo:
                    h.wait_ge(sems[k], v)
            self.recs[e].append(emit)

    def raw(self, eng, fn):
        self.recs[eng].append(fn)

    def emit_all(self):
        nc = self.nc
        recs = self.recs
        with nc.Block() as block:
            @block.sync
            def _(h):
                for f in recs["sp"]:
                    f(h)

            @block.scalar
            def _(h):
                for f in recs["act"]:
                    f(h)

            @block.vector
            def _(h):
                for f in recs["dve"]:
                    f(h)

            @block.tensor
            def _(h):
                for f in recs["pe"]:
                    f(h)

            @block.gpsimd
            def _(h):
                for f in recs["pool"]:
                    f(h)


def _build_program():
    nc = bass.Bass("TRN2", target_bir_lowering=False)

    def din(name, shape, dt):
        return nc.dram_tensor(name, list(shape), dt, kind="ExternalInput").ap()

    x_ap = din("x", (S, D), F32)
    wq_ap = din("wq", (D, NH * HD), F32)
    wk_ap = din("wk", (D, NKV * HD), F32)
    wv_ap = din("wv", (D, NKV * HD), F32)
    wo_ap = din("wo", (NH * HD, D), F32)
    qs_ap = din("qs", (128, HD), F32)
    ks_ap = din("ks", (128, HD), F32)
    gbt_ap = din("gbt", (128, 64 * 2048), U8)
    ubt_ap = din("ubt", (128, 64 * 2048), U8)
    dbt_ap = din("dbt", (128, 16 * 8192), U8)
    gsx_ap = din("gsx", (128, 64 * 2048), F32)
    usx_ap = din("usx", (128, 64 * 2048), F32)
    dsx_ap = din("dsx", (128, 16 * 8192), F32)
    cos_ap = din("cosb", (128, 16 * 64), BF16)
    sin_ap = din("sinb", (128, 16 * 64), BF16)
    m1_ap = din("m1", (128, 128), F32)
    m2_ap = din("m2", (128, 128), F32)
    id_ap = din("ident", (128, 128), BF16)
    out_ap = nc.dram_tensor("out", [S, D], F32, kind="ExternalOutput").ap()

    q_sc = nc.dram_tensor("q_sc", [S, D], BF16, kind="Internal").ap()
    ao_sc = nc.dram_tensor("ao_sc", [S, D], BF16, kind="Internal").ap()
    s_sc = nc.dram_tensor("s_sc", [S, D], F32, kind="Internal").ap()
    int_sc = nc.dram_tensor("int_sc", [MLP, S], BF16, kind="Internal").ap()

    root = ExitStack()
    with root:
        p = Prog(nc, root)

        def sb(ctx, name, shape, dt, side=None):
            if side:
                return ctx.enter_context(nc.sbuf_tensor(name, shape, dt, side=side))
            return ctx.enter_context(nc.sbuf_tensor(name, shape, dt))

        def ps(ctx, name, shape, dt):
            return ctx.enter_context(nc.psum_tensor(name, shape, dt))

        ctxK = ExitStack()
        ident = sb(ctxK, "ident_s", [128, 128], BF16, side="right")
        m1 = sb(ctxK, "m1_s", [128, 128], F32, side="right")
        m2 = sb(ctxK, "m2_s", [128, 128], F32, side="right")
        cosb = sb(ctxK, "cos_s", [128, 1024], BF16, side="right")
        sinb = sb(ctxK, "sin_s", [128, 1024], BF16, side="right")
        qsr = sb(ctxK, "qs_s", [128, HD], F32, side="right")
        ksr = sb(ctxK, "ks_s", [128, HD], F32, side="right")
        c_eps_ln = sb(ctxK, "c_eps_ln", [128, 1], F32, side="right")
        c_eps_q = sb(ctxK, "c_eps_q", [128, 1], F32, side="right")
        c_eps_k = sb(ctxK, "c_eps_k", [128, 1], F32, side="right")
        p.op("dve", lambda h: h.memset(c_eps_ln[:], 1e-5), writes=[c_eps_ln])
        p.op("dve", lambda h: h.memset(c_eps_q[:], 2.56e-4), writes=[c_eps_q])
        p.op("dve", lambda h: h.memset(c_eps_k[:], 1e-6), writes=[c_eps_k])
        for t, apx in ((ident, id_ap), (m1, m1_ap), (m2, m2_ap), (cosb, cos_ap),
                       (sinb, sin_ap), (qsr, qs_ap), (ksr, ks_ap)):
            p.dma("sp", t[:], apx, writes=[t])

        ctxA = ExitStack()
        xnT = [sb(ctxA, f"xnT{c}", [128, S], BF16) for c in range(16)]

        with ExitStack() as cl:
            xin = [sb(cl, f"ln_x{i}", [128, D], F32) for i in range(2)]
            sqs = [sb(cl, f"ln_sq{i}", [128, D], F32) for i in range(2)]
            xnt = [sb(cl, f"ln_xn{i}", [128, D], BF16) for i in range(2)]
            st1 = sb(cl, "ln_s1", [128, 2], F32)
            st2 = sb(cl, "ln_s2", [128, 2], F32)
            mean = sb(cl, "ln_mean", [128, 1], F32)
            var = sb(cl, "ln_var", [128, 1], F32)
            rstd = sb(cl, "ln_rstd", [128, 1], F32)
            ptr = [ps(cl, f"ln_ptr{i}", [128, 1024], BF16) for i in range(2)]
            for tt in range(16):
                xi = xin[tt % 2]
                sq = sqs[tt % 2]
                xo = xnt[tt % 2]
                p.dma("sp", xi[:], x_ap[128 * tt:128 * (tt + 1), :], writes=[xi])
                p.op("dve", lambda h, xi=xi: h.tensor_reduce(
                    out=st1[:, 0:1], in_=xi[:], axis=mybir.AxisListType.X, op=ALU.add),
                    reads=[xi], writes=[st1])
                p.op("act", lambda h, xi=xi, sq=sq: h.activation(
                    out=sq[:], in_=xi[:], func=AF.Square, accum_out=st2[:, 0:1]),
                    reads=[xi], writes=[sq, st2])
                p.op("dve", lambda h: h.tensor_scalar(
                    out=mean[:], in0=st1[:, 0:1], scalar1=1.0 / D, scalar2=None, op0=ALU.mult),
                    reads=[st1], writes=[mean])
                p.op("act", lambda h: h.activation(
                    out=var[:], in_=mean[:], func=AF.Square, scale=-1.0),
                    reads=[mean], writes=[var])
                p.op("dve", lambda h: h.scalar_tensor_tensor(
                    out=var[:], in0=st2[:, 0:1], scalar=1.0 / D, in1=var[:],
                    op0=ALU.mult, op1=ALU.subtract),
                    reads=[st2, var], writes=[var])
                p.op("act", lambda h: h.activation(
                    out=rstd[:], in_=var[:], func=AF.Sqrt, bias=c_eps_ln[:]),
                    reads=[var, c_eps_ln], writes=[rstd])
                p.op("dve", lambda h: h.reciprocal(out=rstd[:], in_=rstd[:]),
                     reads=[rstd], writes=[rstd])
                p.op("dve", lambda h, xi=xi, xo=xo: h.tensor_scalar(
                    out=xo[:], in0=xi[:], scalar1=mean[:], scalar2=rstd[:],
                    op0=ALU.subtract, op1=ALU.mult),
                    reads=[xi, mean, rstd], writes=[xo])
                for kc in range(16):
                    c = kc if kc < 8 else kc - 8
                    off = 256 * c + (0 if kc < 8 else 1)
                    pt = ptr[kc % 2]
                    p.op("pe", lambda h, xo=xo, off=off, pt=pt: h.transpose(
                        pt[:, 0:128], xo[:, off:off + 255:2], ident[:]),
                        reads=[xo, ident], writes=[pt])
                    p.op("dve", lambda h, pt=pt, kc=kc, tt=tt: h.tensor_copy(
                        out=xnT[kc][:, 128 * tt:128 * (tt + 1)], in_=pt[:, 0:128]),
                        reads=[pt], writes=[xnT[kc]])
            p.barrier()

        with ExitStack() as cq:
            wqs = [sb(cq, f"wq{c}", [128, D], BF16) for c in range(16)]
            for c in range(16):
                p.dma("pool", wqs[c][:], wq_ap[128 * c:128 * (c + 1), :], writes=[wqs[c]])
            psq2 = [ps(cq, f"psq{i}", [128, 2048], F32) for i in range(2)]
            qn = sb(cq, "qn", [128, D], F32)
            qt = [sb(cq, f"qt{i}", [128, D], BF16) for i in range(2)]
            ssq = sb(cq, "q_ssq", [128, 8], F32)
            rst = sb(cq, "q_rst", [128, 8], F32)
            sqsc = sb(cq, "q_sqsc", [128, D], F32)
            r0 = sb(cq, "q_r0", [128, 8 * 64], F32)
            r1 = sb(cq, "q_r1", [128, 8 * 64], F32)
            for tt in range(16):
                psq = psq2[tt % 2]
                items = []
                for kc in range(16):
                    for qb in range(4):
                        def bld(h, kc=kc, qb=qb, tt=tt, psq=psq):
                            return h.matmul(psq[:, 512 * qb:512 * (qb + 1)],
                                            lhsT=xnT[kc][:, 128 * tt:128 * (tt + 1)],
                                            rhs=wqs[kc][:, 512 * qb:512 * (qb + 1)],
                                            start=(kc == 0), stop=(kc == 15))
                        items.append((bld, (xnT[kc], wqs[kc])))
                p.mm_group(items, [f"{psq.name}_b{qb}" for qb in range(4)])
                for hh in range(8):
                    p.op("act", lambda h, hh=hh, psq=psq: h.activation(
                        out=sqsc[:, 256 * hh:256 * (hh + 1)],
                        in_=psq[:, 256 * hh:256 * (hh + 1)], func=AF.Square,
                        accum_out=ssq[:, hh:hh + 1]),
                        reads=[f"{psq.name}_b{hh // 2}"], writes=[sqsc, ssq])
                p.op("act", lambda h: h.activation(
                    out=rst[:], in_=ssq[:], func=AF.Sqrt, bias=c_eps_q[:]),
                    reads=[ssq, c_eps_q], writes=[rst])
                p.op("dve", lambda h: h.reciprocal(out=rst[:], in_=rst[:]),
                     reads=[rst], writes=[rst])
                for hh in range(8):
                    p.op("dve", lambda h, hh=hh, psq=psq: h.scalar_tensor_tensor(
                        out=qn[:, 256 * hh:256 * (hh + 1)],
                        in0=psq[:, 256 * hh:256 * (hh + 1)],
                        scalar=rst[:, hh:hh + 1], in1=qsr[:],
                        op0=ALU.mult, op1=ALU.mult),
                        reads=[f"{psq.name}_b{hh // 2}", rst, qsr], writes=[qn])
                qo = qt[tt % 2]
                c_t = cosb[:, 64 * tt:64 * (tt + 1)]
                s_t = sinb[:, 64 * tt:64 * (tt + 1)]
                x0 = qn[:].rearrange("p (h d) -> p h d", d=256)[:, :, 0:64]
                x1 = qn[:].rearrange("p (h d) -> p h d", d=256)[:, :, 64:128]
                cB = c_t.unsqueeze(1).broadcast_to([128, 8, 64])
                sB = s_t.unsqueeze(1).broadcast_to([128, 8, 64])
                r0v = r0[:].rearrange("p (h d) -> p h d", d=64)
                r1v = r1[:].rearrange("p (h d) -> p h d", d=64)
                qe = qo[:].rearrange("p (h d) -> p h d", d=256)[:, :, 0:64]
                qo_ = qo[:].rearrange("p (h d) -> p h d", d=256)[:, :, 64:128]
                qrest = qo[:].rearrange("p (h d) -> p h d", d=256)[:, :, 128:256]
                xrest = qn[:].rearrange("p (h d) -> p h d", d=256)[:, :, 128:256]
                p.op("dve", lambda h, x0=x0, cB=cB, r0v=r0v: h.tensor_tensor(
                    out=r0v, in0=x0, in1=cB, op=ALU.mult), reads=[qn, cosb], writes=[r0])
                p.op("dve", lambda h, x1=x1, sB=sB, r1v=r1v: h.tensor_tensor(
                    out=r1v, in0=x1, in1=sB, op=ALU.mult), reads=[qn, sinb], writes=[r1])
                p.op("dve", lambda h, r0v=r0v, r1v=r1v, qe=qe: h.tensor_tensor(
                    out=qe, in0=r0v, in1=r1v, op=ALU.subtract),
                    reads=[r0, r1], writes=[qo])
                p.op("dve", lambda h, x0=x0, sB=sB, r0v=r0v: h.tensor_tensor(
                    out=r0v, in0=x0, in1=sB, op=ALU.mult), reads=[qn, sinb], writes=[r0])
                p.op("dve", lambda h, x1=x1, cB=cB, r1v=r1v: h.tensor_tensor(
                    out=r1v, in0=x1, in1=cB, op=ALU.mult), reads=[qn, cosb], writes=[r1])
                p.op("dve", lambda h, r0v=r0v, r1v=r1v, qo_=qo_: h.tensor_tensor(
                    out=qo_, in0=r0v, in1=r1v, op=ALU.add),
                    reads=[r0, r1], writes=[qo])
                p.op("dve", lambda h, xrest=xrest, qrest=qrest: h.tensor_copy(
                    out=qrest, in_=xrest), reads=[qn], writes=[qo])
                p.dma("sp", q_sc[128 * tt:128 * (tt + 1), :], qo[:], reads=[qo],
                      writes=["q_sc"])
            p.barrier()

        ctxB = ExitStack()
        kT = [sb(ctxB, f"kT{c}", [128, S], BF16) for c in range(8)]
        Vt = [sb(ctxB, f"V{t}", [128, NKV * HD], BF16) for t in range(16)]

        with ExitStack() as ck:
            wks = [sb(ck, f"wk{c}", [128, NKV * HD], BF16) for c in range(16)]
            for c in range(16):
                p.dma("pool", wks[c][:], wk_ap[128 * c:128 * (c + 1), :], writes=[wks[c]])
            psk2 = [ps(ck, f"psk{i}", [128, 1024], F32) for i in range(2)]
            kn = sb(ck, "kn", [128, 1024], F32)
            kt_ = sb(ck, "kt_", [128, 1024], BF16)
            kssq = sb(ck, "k_ssq", [128, 4], F32)
            krst = sb(ck, "k_rst", [128, 4], F32)
            ksqsc = sb(ck, "k_sqsc", [128, 1024], F32)
            kr0 = sb(ck, "k_r0", [128, 4 * 64], F32)
            kr1 = sb(ck, "k_r1", [128, 4 * 64], F32)
            ptk = [ps(ck, f"ptk{i}", [128, 1024], BF16) for i in range(2)]
            for tt in range(16):
                psk = psk2[tt % 2]
                items = []
                for kc in range(16):
                    for kb in range(2):
                        def bld(h, kc=kc, kb=kb, tt=tt, psk=psk):
                            return h.matmul(psk[:, 512 * kb:512 * (kb + 1)],
                                            lhsT=xnT[kc][:, 128 * tt:128 * (tt + 1)],
                                            rhs=wks[kc][:, 512 * kb:512 * (kb + 1)],
                                            start=(kc == 0), stop=(kc == 15))
                        items.append((bld, (xnT[kc], wks[kc])))
                p.mm_group(items, [f"{psk.name}_b{kb}" for kb in range(2)])
                for hh in range(4):
                    p.op("act", lambda h, hh=hh, psk=psk: h.activation(
                        out=ksqsc[:, 256 * hh:256 * (hh + 1)],
                        in_=psk[:, 256 * hh:256 * (hh + 1)], func=AF.Square,
                        accum_out=kssq[:, hh:hh + 1]),
                        reads=[f"{psk.name}_b{hh // 2}"], writes=[ksqsc, kssq])
                p.op("act", lambda h: h.activation(
                    out=krst[:], in_=kssq[:], func=AF.Sqrt, scale=1.0 / 256, bias=c_eps_k[:]),
                    reads=[kssq, c_eps_k], writes=[krst])
                p.op("dve", lambda h: h.reciprocal(out=krst[:], in_=krst[:]),
                     reads=[krst], writes=[krst])
                for hh in range(4):
                    p.op("dve", lambda h, hh=hh, psk=psk: h.scalar_tensor_tensor(
                        out=kn[:, 256 * hh:256 * (hh + 1)],
                        in0=psk[:, 256 * hh:256 * (hh + 1)],
                        scalar=krst[:, hh:hh + 1], in1=ksr[:],
                        op0=ALU.mult, op1=ALU.mult),
                        reads=[f"{psk.name}_b{hh // 2}", krst, ksr], writes=[kn])
                c_t = cosb[:, 64 * tt:64 * (tt + 1)]
                s_t = sinb[:, 64 * tt:64 * (tt + 1)]
                x0 = kn[:].rearrange("p (h d) -> p h d", d=256)[:, :, 0:64]
                x1 = kn[:].rearrange("p (h d) -> p h d", d=256)[:, :, 64:128]
                cB = c_t.unsqueeze(1).broadcast_to([128, 4, 64])
                sB = s_t.unsqueeze(1).broadcast_to([128, 4, 64])
                r0v = kr0[:].rearrange("p (h d) -> p h d", d=64)
                r1v = kr1[:].rearrange("p (h d) -> p h d", d=64)
                ke = kt_[:].rearrange("p (h d) -> p h d", d=256)[:, :, 0:64]
                ko = kt_[:].rearrange("p (h d) -> p h d", d=256)[:, :, 64:128]
                krest = kt_[:].rearrange("p (h d) -> p h d", d=256)[:, :, 128:256]
                xrest = kn[:].rearrange("p (h d) -> p h d", d=256)[:, :, 128:256]
                p.op("dve", lambda h, x0=x0, cB=cB, r0v=r0v: h.tensor_tensor(
                    out=r0v, in0=x0, in1=cB, op=ALU.mult), reads=[kn, cosb], writes=[kr0])
                p.op("dve", lambda h, x1=x1, sB=sB, r1v=r1v: h.tensor_tensor(
                    out=r1v, in0=x1, in1=sB, op=ALU.mult), reads=[kn, sinb], writes=[kr1])
                p.op("dve", lambda h, r0v=r0v, r1v=r1v, ke=ke: h.tensor_tensor(
                    out=ke, in0=r0v, in1=r1v, op=ALU.subtract),
                    reads=[kr0, kr1], writes=[kt_])
                p.op("dve", lambda h, x0=x0, sB=sB, r0v=r0v: h.tensor_tensor(
                    out=r0v, in0=x0, in1=sB, op=ALU.mult), reads=[kn, sinb], writes=[kr0])
                p.op("dve", lambda h, x1=x1, cB=cB, r1v=r1v: h.tensor_tensor(
                    out=r1v, in0=x1, in1=cB, op=ALU.mult), reads=[kn, cosb], writes=[kr1])
                p.op("dve", lambda h, r0v=r0v, r1v=r1v, ko=ko: h.tensor_tensor(
                    out=ko, in0=r0v, in1=r1v, op=ALU.add),
                    reads=[kr0, kr1], writes=[kt_])
                p.op("dve", lambda h, xrest=xrest, krest=krest: h.tensor_copy(
                    out=krest, in_=xrest), reads=[kn], writes=[kt_])
                for dc in range(8):
                    pt = ptk[dc % 2]
                    p.op("pe", lambda h, dc=dc, pt=pt: h.transpose(
                        pt[:, 0:128], kt_[:, 128 * dc:128 * (dc + 1)], ident[:]),
                        reads=[kt_, ident], writes=[pt])
                    p.op("dve", lambda h, pt=pt, dc=dc, tt=tt: h.tensor_copy(
                        out=kT[dc][:, 128 * tt:128 * (tt + 1)], in_=pt[:, 0:128]),
                        reads=[pt], writes=[kT[dc]])
            p.barrier()

        with ExitStack() as cv:
            wvs = [sb(cv, f"wv{c}", [128, NKV * HD], BF16) for c in range(16)]
            for c in range(16):
                p.dma("pool", wvs[c][:], wv_ap[128 * c:128 * (c + 1), :], writes=[wvs[c]])
            psv2 = [ps(cv, f"psv{i}", [128, 1024], F32) for i in range(2)]
            for tt in range(16):
                psv = psv2[tt % 2]
                items = []
                for kc in range(16):
                    for kb in range(2):
                        def bld(h, kc=kc, kb=kb, tt=tt, psv=psv):
                            return h.matmul(psv[:, 512 * kb:512 * (kb + 1)],
                                            lhsT=xnT[kc][:, 128 * tt:128 * (tt + 1)],
                                            rhs=wvs[kc][:, 512 * kb:512 * (kb + 1)],
                                            start=(kc == 0), stop=(kc == 15))
                        items.append((bld, (xnT[kc], wvs[kc])))
                p.mm_group(items, [f"{psv.name}_b{kb}" for kb in range(2)])
                p.op("act", lambda h, tt=tt, psv=psv: h.activation(
                    out=Vt[tt][:], in_=psv[:], func=AF.Copy),
                    reads=[f"{psv.name}_b0", f"{psv.name}_b1"], writes=[Vt[tt]])
            p.barrier()

        with ExitStack() as ca:
            qin = [sb(ca, f"at_q{i}", [128, D], BF16) for i in range(2)]
            qTt = sb(ca, "at_qT", [128, D], BF16)
            sS = [sb(ca, f"at_sS{i}", [128, 1152], F32) for i in range(2)]
            pbf = [sb(ca, f"at_p{i}", [128, 1152], BF16) for i in range(2)]
            pTs = [sb(ca, f"at_pT{i}", [128, 1152], BF16) for i in range(2)]
            mx = sb(ca, "at_mx", [128, 2], F32)
            ngm = sb(ca, "at_ngm", [128, 2], F32)
            rsum = sb(ca, "at_rs", [128, 2], F32)
            rcp = sb(ca, "at_rcp", [128, 2], F32)
            aot = [sb(ca, f"at_ao{i}", [128, D], BF16) for i in range(2)]
            psS = [ps(ca, f"at_psS{i}", [128, 1536], F32) for i in range(2)]
            psT = ps(ca, "at_psT", [128, 1024], BF16)
            psO = ps(ca, "at_psO", [128, 512], F32)

            def scores(tt, hh, klo, w):
                hk = hh // 2
                pS = psS[hh % 2]
                items = []
                for dh in range(2):
                    o = 0
                    while o < w:
                        bw = min(512, w - o)
                        def bld(h, hh=hh, dh=dh, o=o, bw=bw, klo=klo, hk=hk,
                                pS=pS):
                            return h.matmul(
                                pS[:, o:o + bw],
                                lhsT=qTt[:, 128 * (2 * hh + dh):128 * (2 * hh + dh + 1)],
                                rhs=kT[2 * hk + dh][:, klo + o:klo + o + bw],
                                start=(dh == 0), stop=(dh == 1))
                        items.append((bld, (qTt, kT[2 * hk + dh])))
                        o += bw
                p.mm_group(items, [f"{pS.name}_{i}" for i in range((w + 511) // 512)])

            def softmax_pt_pv(tt, hh, klo, w, ao):
                par = hh % 2
                hk = hh // 2
                pS = psS[par]
                sSx = sS[par]
                pb = pbf[par]
                pT = pTs[par]
                nkt = w // 128
                nblk = (w + 511) // 512
                mxs = mx[:, par:par + 1]
                ngs = ngm[:, par:par + 1]
                rss = rsum[:, par:par + 1]
                rcs = rcp[:, par:par + 1]
                p.op("act", lambda h, w=w, pS=pS, sSx=sSx: h.activation(
                    out=sSx[:, 0:w], in_=pS[:, 0:w], func=AF.Tanh, scale=0.02),
                    reads=[f"{pS.name}_{i}" for i in range(nblk)], writes=[sSx])
                p.op("dve", lambda h, w=w, sSx=sSx: h.tensor_tensor(
                    out=sSx[:, w - 128:w], in0=sSx[:, w - 128:w], in1=m1[:],
                    op=ALU.add), reads=[sSx, m1], writes=[sSx])
                if nkt == 9:
                    p.op("dve", lambda h, sSx=sSx: h.tensor_tensor(
                        out=sSx[:, 0:128], in0=sSx[:, 0:128], in1=m2[:],
                        op=ALU.add), reads=[sSx, m2], writes=[sSx])
                p.op("dve", lambda h, w=w, sSx=sSx, mxs=mxs: h.tensor_reduce(
                    out=mxs, in_=sSx[:, 0:w], axis=mybir.AxisListType.X, op=ALU.max),
                    reads=[sSx], writes=["at_mx"])
                p.op("dve", lambda h, mxs=mxs, ngs=ngs: h.tensor_scalar(
                    out=ngs, in0=mxs, scalar1=-50.0, scalar2=None, op0=ALU.mult),
                    reads=["at_mx"], writes=["at_ngm"])
                p.op("act", lambda h, w=w, sSx=sSx, pb=pb, ngs=ngs, rss=rss:
                     h.activation(out=pb[:, 0:w], in_=sSx[:, 0:w], func=AF.Exp,
                                  scale=50.0, bias=ngs, accum_out=rss),
                     reads=[sSx, "at_ngm"], writes=[pb, "at_rs"])
                p.op("dve", lambda h, rss=rss, rcs=rcs: h.reciprocal(
                    out=rcs, in_=rss), reads=["at_rs"], writes=["at_rcp"])
                n1 = min(nkt, 8)
                for kb in range(n1):
                    p.op("pe", lambda h, kb=kb, pb=pb: h.transpose(
                        psT[:, 128 * kb:128 * (kb + 1)], pb[:, 128 * kb:128 * (kb + 1)],
                        ident[:]), reads=[pb, ident], writes=["at_psT"])
                p.op("act", lambda h, n1=n1, pT=pT: h.activation(
                    out=pT[:, 0:128 * n1], in_=psT[:, 0:128 * n1], func=AF.Copy),
                    reads=["at_psT"], writes=[pT])
                if nkt == 9:
                    p.op("pe", lambda h, pb=pb: h.transpose(
                        psT[:, 0:128], pb[:, 1024:1152], ident[:]),
                        reads=[pb, ident], writes=["at_psT"])
                    p.op("act", lambda h, pT=pT: h.activation(
                        out=pT[:, 1024:1152], in_=psT[:, 0:128], func=AF.Copy),
                        reads=["at_psT"], writes=[pT])
                items = []
                for kb in range(nkt):
                    kt_i = klo // 128 + kb
                    def bld(h, kb=kb, kt_i=kt_i, hk=hk, nkt=nkt, pT=pT):
                        return h.matmul(
                            psO[:, 0:256], lhsT=pT[:, 128 * kb:128 * (kb + 1)],
                            rhs=Vt[kt_i][:, 256 * hk:256 * (hk + 1)],
                            start=(kb == 0), stop=(kb == nkt - 1))
                    items.append((bld, (pT, Vt[kt_i])))
                p.mm_group(items, "at_psO")
                p.op("dve", lambda h, hh=hh, ao=ao, rcs=rcs: h.tensor_scalar(
                    out=ao[:, 256 * hh:256 * (hh + 1)], in0=psO[:, 0:256],
                    scalar1=rcs, scalar2=None, op0=ALU.mult),
                    reads=["at_psO", "at_rcp"], writes=[ao])

            for tt in range(16):
                qi = qin[tt % 2]
                ao = aot[tt % 2]
                p.dma("sp", qi[:], q_sc[128 * tt:128 * (tt + 1), :], reads=["q_sc"],
                      writes=[qi])
                for rnd in range(2):
                    for dc8 in range(8):
                        dc = 8 * rnd + dc8
                        p.op("pe", lambda h, qi=qi, dc=dc, dc8=dc8: h.transpose(
                            psT[:, 128 * dc8:128 * (dc8 + 1)],
                            qi[:, 128 * dc:128 * (dc + 1)], ident[:]),
                            reads=[qi, ident], writes=["at_psT"])
                    p.op("act", lambda h, rnd=rnd: h.activation(
                        out=qTt[:, 1024 * rnd:1024 * (rnd + 1)], in_=psT[:],
                        func=AF.Copy), reads=["at_psT"], writes=[qTt])
                klo = max(0, 128 * (tt - 8))
                w = 128 * (tt + 1) - klo
                scores(tt, 0, klo, w)
                for hh in range(8):
                    if hh + 1 < 8:
                        scores(tt, hh + 1, klo, w)
                    softmax_pt_pv(tt, hh, klo, w, ao)
                p.dma("pool", ao_sc[128 * tt:128 * (tt + 1), :], ao[:], reads=[ao],
                      writes=["ao_sc"])
            p.barrier()
        ctxB.close()
        p.barrier()

        with ExitStack() as cs:
            wos = [sb(cs, f"wo{c}", [128, D], BF16) for c in range(16)]
            for c in range(16):
                p.dma("pool", wos[c][:], wo_ap[128 * c:128 * (c + 1), :], writes=[wos[c]])
            aoin = [sb(cs, f"sp_ao{i}", [128, D], BF16) for i in range(2)]
            aoT = sb(cs, "sp_aoT", [128, D], BF16)
            sout = [sb(cs, f"sp_s{i}", [128, D], F32) for i in range(2)]
            pss2 = [ps(cs, f"sp_pss{i}", [128, 2048], F32) for i in range(2)]
            p.dma("sp", aoin[0][:], ao_sc[0:128, :], reads=["ao_sc"], writes=[aoin[0]])
            for tt in range(16):
                ai = aoin[tt % 2]
                so = sout[tt % 2]
                pss = pss2[tt % 2]
                if tt + 1 < 16:
                    p.dma("sp", aoin[(tt + 1) % 2][:],
                          ao_sc[128 * (tt + 1):128 * (tt + 2), :],
                          reads=["ao_sc"], writes=[aoin[(tt + 1) % 2]])
                pav = pss[:, 0:512].bitcast(BF16)
                tname = f"{pss.name}_T"
                for rnd in range(2):
                    for dc8 in range(8):
                        dc = 8 * rnd + dc8
                        p.op("pe", lambda h, ai=ai, dc=dc, dc8=dc8, pav=pav:
                             h.transpose(pav[:, 128 * dc8:128 * (dc8 + 1)],
                                         ai[:, 128 * dc:128 * (dc + 1)], ident[:]),
                             reads=[ai, ident], writes=[tname])
                    p.op("act", lambda h, rnd=rnd, pav=pav: h.activation(
                        out=aoT[:, 1024 * rnd:1024 * (rnd + 1)], in_=pav[:],
                        func=AF.Copy), reads=[tname], writes=[aoT])
                items = []
                for dc in range(16):
                    for hb in range(4):
                        def bld(h, dc=dc, hb=hb, pss=pss):
                            return h.matmul(pss[:, 512 * hb:512 * (hb + 1)],
                                            lhsT=aoT[:, 128 * dc:128 * (dc + 1)],
                                            rhs=wos[dc][:, 512 * hb:512 * (hb + 1)],
                                            start=(dc == 0), stop=(dc == 15))
                        items.append((bld, (aoT, wos[dc])))
                p.mm_group(items, [f"{pss.name}_{hb}" for hb in range(4)] + [tname])
                p.op("act", lambda h, so=so, pss=pss: h.activation(
                    out=so[:], in_=pss[:], func=AF.Copy),
                    reads=[f"{pss.name}_{hb}" for hb in range(4)], writes=[so])
                p.dma("pool", s_sc[128 * tt:128 * (tt + 1), :], so[:], reads=[so],
                      writes=["s_sc"])
            p.barrier()
        ctxK.close()
        p.barrier()

        NBW = 256
        NGRP = MLP // NBW
        with ExitStack() as cg:
            pk = {m: [sb(cg, f"gu_pk_{m}{i}", [128, 8 * NBW], U8) for i in range(2)]
                  for m in ("g", "u")}
            scx = {m: [sb(cg, f"gu_sc_{m}{i}", [128, 8 * NBW], BF16) for i in range(2)]
                   for m in ("g", "u")}
            cch = {m: [sb(cg, f"gu_C{m}{b}", [128, 16 * NBW], BF16)
                       for b in range(3)] for m in ("g", "u")}
            t7c = sb(cg, "gu_t7", [128, 16 * NBW], U8)
            s8c = sb(cg, "gu_s8", [128, 16 * NBW], U8)
            mgc = sb(cg, "gu_mg", [128, 16 * NBW], BF16)
            sgc = sb(cg, "gu_sg", [128, 16 * NBW], BF16)
            gbuf = sb(cg, "gu_gb", [128, 2048], F32)
            ubuf = sb(cg, "gu_ub", [128, 2048], F32)
            gsq = sb(cg, "gu_gsq", [128, 2048], F32)
            zb = sb(cg, "gu_z", [128, 2048], F32)
            itt = [sb(cg, f"gu_it{i}", [128, 2048], BF16) for i in range(2)]
            pgu = [ps(cg, f"gu_ps{i}", [128, 512], F32) for i in range(8)]

            def decode(mat, g):
                pkt = pk[mat][g % 2]
                sct = scx[mat][g % 2]
                cc = cch[mat][g % 3]
                p16 = pkt[:].bitcast(I16)
                tlo = t7c[:, 0:2048].bitcast(I16)
                thi = t7c[:, 2048:4096].bitcast(I16)
                slo = s8c[:, 0:2048].bitcast(I16)
                shi = s8c[:, 2048:4096].bitcast(I16)
                p.op("dve", lambda h, p16=p16, tlo=tlo: h.tensor_scalar(
                    out=tlo, in0=p16, scalar1=0x0707, scalar2=None,
                    op0=ALU.bitwise_and), reads=[pkt], writes=[t7c])
                p.op("dve", lambda h, p16=p16, thi=thi: h.tensor_scalar(
                    out=thi, in0=p16, scalar1=4, scalar2=0x0707,
                    op0=ALU.logical_shift_right, op1=ALU.bitwise_and),
                    reads=[pkt], writes=[t7c])
                p.op("dve", lambda h, p16=p16, slo=slo: h.tensor_scalar(
                    out=slo, in0=p16, scalar1=0x0808, scalar2=None,
                    op0=ALU.bitwise_and), reads=[pkt], writes=[s8c])
                p.op("dve", lambda h, p16=p16, shi=shi: h.tensor_scalar(
                    out=shi, in0=p16, scalar1=4, scalar2=0x0808,
                    op0=ALU.logical_shift_right, op1=ALU.bitwise_and),
                    reads=[pkt], writes=[s8c])
                bb16 = mgc[:].bitcast(I16)
                p.op("act", lambda h, bb16=bb16: h.activation(
                    out=bb16, in_=t7c[:], func=AF.Copy, scale=64.0, bias=16128.0),
                    reads=[t7c], writes=[mgc])
                p.op("act", lambda h: h.activation(
                    out=sgc[:], in_=s8c[:], func=AF.Copy, scale=-0.25, bias=1.0),
                    reads=[s8c], writes=[sgc])
                p.op("dve", lambda h: h.scalar_tensor_tensor(
                    out=mgc[:], in0=mgc[:], scalar=1.0, in1=mgc[:],
                    op0=ALU.min, op1=ALU.add), reads=[mgc], writes=[mgc])
                p.op("dve", lambda h: h.tensor_scalar(
                    out=mgc[:], in0=mgc[:], scalar1=-1.0, scalar2=None,
                    op0=ALU.add), reads=[mgc], writes=[mgc])
                p.op("dve", lambda h: h.tensor_tensor(
                    out=mgc[:], in0=mgc[:], in1=sgc[:], op=ALU.mult),
                    reads=[mgc, sgc], writes=[mgc])
                scb = sct[:].unsqueeze(1).broadcast_to([128, 2, 2048])
                p.op("dve", lambda h, cc=cc, scb=scb: h.tensor_tensor(
                    out=cc[:].rearrange("p (pl n) -> p pl n", pl=2),
                    in0=mgc[:].rearrange("p (pl n) -> p pl n", pl=2),
                    in1=scb, op=ALU.mult),
                    reads=[mgc, sct], writes=[cc])

            def load_group(mat, ap_w, ap_s, g):
                pkt = pk[mat][g % 2]
                sct = scx[mat][g % 2]
                p.dma("sp", pkt[:], ap_w[:, 2048 * g:2048 * (g + 1)],
                      writes=[pkt])
                p.dma("pool", sct[:], ap_s[:, 2048 * g:2048 * (g + 1)],
                      writes=[sct])

            MATS = (("g", gbt_ap, gsx_ap), ("u", ubt_ap, usx_ap))
            for mat, ap_w, ap_s in MATS:
                load_group(mat, ap_w, ap_s, 0)
                decode(mat, 0)
            MATS2 = (("g", gbuf), ("u", ubuf))
            for gpre in (0, 1):
                for mat, ap_w, ap_s in MATS:
                    load_group(mat, ap_w, ap_s, gpre)
                    decode(mat, gpre)
            for g in range(NGRP):
                buf = g % 3
                if g + 2 < NGRP:
                    for mat, ap_w, ap_s in MATS:
                        load_group(mat, ap_w, ap_s, g + 2)
                        decode(mat, g + 2)
                for nt in range(2):
                    cglob = 2 * g + nt
                    io = itt[nt]
                    for mat, obuf in MATS2:
                        cc = cch[mat][buf]
                        boff = 4 * (0 if mat == "g" else 1)
                        items = []
                        for kc in range(16):
                            pl = 0 if kc < 8 else 1
                            jc = kc % 8
                            for tb in range(4):
                                bank = pgu[boff + tb]
                                def bld(h, cc=cc, jc=jc, pl=pl, nt=nt, tb=tb,
                                        bank=bank, kc=kc):
                                    o = 2048 * pl + NBW * jc + 128 * nt
                                    return h.matmul(
                                        bank[:], lhsT=cc[:, o:o + 128],
                                        rhs=xnT[kc][:, 512 * tb:512 * (tb + 1)],
                                        start=(kc == 0), stop=(kc == 15))
                                items.append((bld, (cc, xnT[kc])))
                        p.mm_group(items, [pgu[boff + tb].name for tb in range(4)])
                        for tb in range(4):
                            bank = pgu[boff + tb]
                            p.op("act", lambda h, bank=bank, tb=tb, obuf=obuf:
                                 h.activation(out=obuf[:, 512 * tb:512 * (tb + 1)],
                                              in_=bank[:], func=AF.Copy),
                                 reads=[bank.name], writes=[obuf])
                    p.op("act", lambda h: h.activation(
                        out=gsq[:], in_=gbuf[:], func=AF.Square),
                        reads=[gbuf], writes=[gsq])
                    p.op("dve", lambda h: h.tensor_scalar(
                        out=gsq[:], in0=gsq[:], scalar1=GB, scalar2=GA,
                        op0=ALU.mult, op1=ALU.add), reads=[gsq], writes=[gsq])
                    p.op("dve", lambda h: h.tensor_tensor(
                        out=zb[:], in0=gsq[:], in1=gbuf[:], op=ALU.mult),
                        reads=[gsq, gbuf], writes=[zb])
                    p.op("act", lambda h: h.activation(
                        out=gsq[:], in_=zb[:], func=AF.Sigmoid),
                        reads=[zb], writes=[gsq])
                    p.op("dve", lambda h: h.tensor_tensor(
                        out=zb[:], in0=gbuf[:], in1=gsq[:], op=ALU.mult),
                        reads=[gbuf, gsq], writes=[zb])
                    p.op("dve", lambda h, io=io: h.tensor_tensor(
                        out=io[:], in0=zb[:], in1=ubuf[:], op=ALU.mult),
                        reads=[zb, ubuf], writes=[io])
                    p.dma("pool", int_sc[128 * cglob:128 * (cglob + 1), :], io[:],
                          reads=[io], writes=["int_sc"])
            p.barrier()
        ctxA.close()
        p.barrier()

        with ExitStack() as cd:
            pkd = [sb(cd, f"dn_pk{i}", [128, 4096], U8) for i in range(2)]
            scd = [sb(cd, f"dn_sc{i}", [128, 4096], BF16) for i in range(2)]
            cdh = [sb(cd, f"dn_C{x}", [128, 4096], BF16) for x in range(3)]
            itr = [sb(cd, f"dn_it{i}", [128, 2048], BF16) for i in range(8)]
            dt7 = sb(cd, "dn_t7", [128, 4096], U8)
            ds8 = sb(cd, "dn_s8", [128, 4096], U8)
            dbb = sb(cd, "dn_bb", [128, 4096], I16)
            dmg = sb(cd, "dn_mg", [128, 4096], BF16)
            dsg = sb(cd, "dn_sg", [128, 4096], BF16)
            acc = [sb(cd, f"acc{t}", [128, 1024], F32) for t in range(16)]
            xf = [sb(cd, f"fn_x{i}", [128, 1024], F32) for i in range(2)]
            sf = [sb(cd, f"fn_s{i}", [128, 1024], F32) for i in range(2)]
            gf = [sb(cd, f"fn_g{i}", [128, 1024], F32) for i in range(2)]
            pdn = [ps(cd, f"dn_ps{i}", [128, 1024], F32) for i in range(4)]

            def dn_load(ss, half):
                pp = pkd[ss % 2]
                sc = scd[ss % 2]
                base = ((ss % 16) * 2 + half) * 4096
                p.dma("pool", pp[:], dbt_ap[:, base:base + 4096], writes=[pp])
                p.dma("pool", sc[:],
                      dsx_ap[:, 8192 * (ss % 16) + 4096 * half:
                             8192 * (ss % 16) + 4096 * (half + 1)],
                      writes=[sc])

            def dn_decode(ss, slot):
                lo_plane = ss < 16
                pp = pkd[ss % 2]
                sc = scd[ss % 2]
                dst = cdh[slot]
                s16 = pp[:].bitcast(I16)
                t16 = dt7[:].bitcast(I16)
                x16 = ds8[:].bitcast(I16)
                if lo_plane:
                    p.op("dve", lambda h, s16=s16, t16=t16: h.tensor_scalar(
                        out=t16, in0=s16, scalar1=0x0707, scalar2=None,
                        op0=ALU.bitwise_and), reads=[pp], writes=[dt7])
                    p.op("dve", lambda h, s16=s16, x16=x16: h.tensor_scalar(
                        out=x16, in0=s16, scalar1=0x0808, scalar2=None,
                        op0=ALU.bitwise_and), reads=[pp], writes=[ds8])
                else:
                    p.op("dve", lambda h, s16=s16, t16=t16: h.tensor_scalar(
                        out=t16, in0=s16, scalar1=4, scalar2=0x0707,
                        op0=ALU.logical_shift_right, op1=ALU.bitwise_and),
                        reads=[pp], writes=[dt7])
                    p.op("dve", lambda h, s16=s16, x16=x16: h.tensor_scalar(
                        out=x16, in0=s16, scalar1=4, scalar2=0x0808,
                        op0=ALU.logical_shift_right, op1=ALU.bitwise_and),
                        reads=[pp], writes=[ds8])
                p.op("act", lambda h: h.activation(
                    out=dbb[:], in_=dt7[:], func=AF.Copy, scale=64.0, bias=16128.0),
                    reads=[dt7], writes=[dbb])
                p.op("act", lambda h: h.activation(
                    out=dsg[:], in_=ds8[:], func=AF.Copy, scale=-0.25, bias=1.0),
                    reads=[ds8], writes=[dsg])
                ab = dbb[:].bitcast(BF16)
                p.op("dve", lambda h, ab=ab: h.tensor_scalar(
                    out=dmg[:], in0=ab, scalar1=1.0, scalar2=None,
                    op0=ALU.min), reads=[dbb], writes=[dmg])
                p.op("dve", lambda h, ab=ab: h.tensor_tensor(
                    out=dmg[:], in0=dmg[:], in1=ab, op=ALU.add),
                    reads=[dmg, dbb], writes=[dmg])
                p.op("dve", lambda h: h.tensor_scalar(
                    out=dmg[:], in0=dmg[:], scalar1=-1.0, scalar2=None,
                    op0=ALU.add), reads=[dmg], writes=[dmg])
                p.op("dve", lambda h: h.tensor_tensor(
                    out=dmg[:], in0=dmg[:], in1=dsg[:], op=ALU.mult),
                    reads=[dmg, dsg], writes=[dmg])
                p.op("pool", lambda h, dst=dst, sc=sc: h.tensor_tensor(
                    out=dst[:], in0=dmg[:], in1=sc[:], op=ALU.mult),
                    reads=[dmg, sc], writes=[dst])

            def dn_iload(ss):
                for ns in range(4):
                    cglob = 4 * ss + ns
                    slot = itr[cglob % 8]
                    p.dma("sp", slot[:], int_sc[128 * cglob:128 * (cglob + 1), :],
                          reads=["int_sc"], writes=[slot])

            def dn_mm(ss, slot):
                cx = cdh[slot]
                for tt in range(16):
                    bank = pdn[tt % 4]
                    items = []
                    for ns in range(4):
                        cglob = 4 * ss + ns
                        sl = itr[cglob % 8]
                        for hb in range(2):
                            def bld(h, tt=tt, ns=ns, hb=hb, bank=bank, sl=sl,
                                    cx=cx):
                                return h.matmul(
                                    bank[:, 512 * hb:512 * (hb + 1)],
                                    lhsT=sl[:, 128 * tt:128 * (tt + 1)],
                                    rhs=cx[:, 1024 * ns + 512 * hb:
                                           1024 * ns + 512 * (hb + 1)],
                                    start=(ns == 0), stop=(ns == 3))
                            items.append((bld, (sl, cx)))
                    p.mm_group(items, bank.name)
                    if ss == 0:
                        p.op("dve", lambda h, bank=bank, tt=tt: h.tensor_copy(
                            out=acc[tt][:], in_=bank[:]),
                            reads=[bank.name], writes=[acc[tt]])
                    else:
                        p.op("dve", lambda h, bank=bank, tt=tt: h.tensor_tensor(
                            out=acc[tt][:], in0=acc[tt][:], in1=bank[:],
                            op=ALU.add),
                            reads=[bank.name, acc[tt]], writes=[acc[tt]])

            def finalize(half):
                for tt in range(16):
                    xi = xf[tt % 2]
                    si = sf[tt % 2]
                    gi = gf[tt % 2]
                    p.dma("act", xi[:],
                          x_ap[128 * tt:128 * (tt + 1),
                               1024 * half:1024 * (half + 1)], writes=[xi])
                    p.dma("act", si[:],
                          s_sc[128 * tt:128 * (tt + 1),
                               1024 * half:1024 * (half + 1)],
                          reads=["s_sc"], writes=[si])
                    p.op("act", lambda h, tt=tt, gi=gi: h.activation(
                        out=gi[:], in_=acc[tt][:], func=AF.Silu),
                        reads=[acc[tt]], writes=[gi])
                    p.op("dve", lambda h, gi=gi, si=si: h.tensor_tensor(
                        out=gi[:], in0=gi[:], in1=si[:], op=ALU.mult),
                        reads=[gi, si], writes=[gi])
                    p.op("dve", lambda h, gi=gi, xi=xi: h.tensor_tensor(
                        out=gi[:], in0=gi[:], in1=xi[:], op=ALU.add),
                        reads=[gi, xi], writes=[gi])
                    p.dma("act",
                          out_ap[128 * tt:128 * (tt + 1),
                                 1024 * half:1024 * (half + 1)], gi[:],
                          reads=[gi], writes=["out_hbm"])

            for half in range(2):
                dn_iload(0)
                dn_load(0, half)
                dn_decode(0, (32 * half) % 3)
                if half == 1:
                    finalize(0)
                for ss in range(32):
                    step = 32 * half + ss
                    if ss + 1 < 32:
                        dn_load(ss + 1, half)
                        dn_decode(ss + 1, (step + 1) % 3)
                        dn_iload(ss + 1)
                    dn_mm(ss, step % 3)
            finalize(1)
            p.barrier()

        p.emit_all()
    return nc


def _host_layout(inputs):
    x = np.asarray(inputs["L0_attn_x"])
    Wq = np.asarray(inputs["L0_attn_Wq"])
    Wk = np.asarray(inputs["L0_attn_Wk"])
    Wv = np.asarray(inputs["L0_attn_Wv"])
    Wo = np.asarray(inputs["L0_attn_Wo"])
    q_scale = np.asarray(inputs["L0_attn_q_scale"])
    k_scale = np.asarray(inputs["L0_attn_k_scale"])
    gW = np.asarray(inputs["L0_ffn_gate_W_packed"])
    gS = np.asarray(inputs["L0_ffn_gate_scales"])
    uW = np.asarray(inputs["L0_ffn_up_W_packed"])
    uS = np.asarray(inputs["L0_ffn_up_scales"])
    dW = np.asarray(inputs["L0_ffn_down_W_packed"])
    dS = np.asarray(inputs["L0_ffn_down_scales"])

    rho = np.concatenate([np.arange(0, D, 2), np.arange(1, D, 2)])
    pi = np.concatenate([np.arange(0, ROT, 2), np.arange(1, ROT, 2),
                         np.arange(ROT, HD)])
    piq = np.concatenate([pi + HD * h for h in range(NH)])
    pik = np.concatenate([pi + HD * h for h in range(NKV)])
    sig = np.concatenate([np.arange(0, MLP, 2), np.arange(1, MLP, 2)])

    shared = {}
    shared["wq"] = np.ascontiguousarray(Wq[rho][:, piq], dtype=np.float32)
    shared["wk"] = np.ascontiguousarray(Wk[rho][:, pik], dtype=np.float32)
    shared["wv"] = np.ascontiguousarray(Wv[rho], dtype=np.float32)
    shared["wo"] = np.ascontiguousarray(Wo, dtype=np.float32)
    shared["qs"] = np.ascontiguousarray(
        np.broadcast_to(q_scale[pi].astype(np.float32), (128, HD)))
    shared["ks"] = np.ascontiguousarray(
        np.broadcast_to(k_scale[pi].astype(np.float32), (128, HD)))

    def bytes_of(Wp):
        return Wp.view(np.uint8).reshape(Wp.shape[0], Wp.shape[1], 4)[:, :, 0]

    def gu_pack(Wp):
        B = bytes_of(Wp)[sig].T
        return np.ascontiguousarray(
            B.reshape(8, 128, 64, 256).transpose(1, 2, 0, 3).reshape(128, 64 * 2048))
    shared["gbt"] = gu_pack(gW)
    shared["ubt"] = gu_pack(uW)
    Dt = bytes_of(dW).T
    shared["dbt"] = np.ascontiguousarray(
        Dt.reshape(16, 4, 128, 2, 1024).transpose(2, 0, 3, 1, 4).reshape(128, 16 * 8192))
    p16 = np.arange(128) // 16
    def gu_exp(Ssc):
        gsT = Ssc[sig].T.astype(np.float32)
        rows = 8 * np.arange(8)[None, :] + p16[:, None]
        B1 = gsT[rows]
        return np.ascontiguousarray(
            B1.reshape(128, 8, 64, 256).transpose(0, 2, 1, 3).reshape(128, 64 * 2048))
    shared["gsx"] = gu_exp(gS)
    shared["usx"] = gu_exp(uS)
    dsT = dS.T.astype(np.float32)
    rows = (32 * np.arange(16)[None, :, None] + 8 * np.arange(4)[None, None, :]
            + p16[:, None, None])
    D1 = dsT[rows]
    D1 = D1.reshape(128, 16, 4, 2, 1024).transpose(0, 1, 3, 2, 4)
    shared["dsx"] = np.ascontiguousarray(D1.reshape(128, 16 * 8192))

    theta = 1.0 / (10000.0 ** (np.arange(0, ROT, 2, dtype=np.float32) / ROT))
    pos = np.arange(S, dtype=np.float32)
    ang = pos[:, None] * theta[None, :]
    cosc = np.cos(ang).astype(np.float32).reshape(16, 128, 64).transpose(1, 0, 2)
    sinc = np.sin(ang).astype(np.float32).reshape(16, 128, 64).transpose(1, 0, 2)
    shared["cosb"] = np.ascontiguousarray(cosc.reshape(128, 1024)).astype(BF)
    shared["sinb"] = np.ascontiguousarray(sinc.reshape(128, 1024)).astype(BF)
    ii = np.arange(128)[:, None]
    jj = np.arange(128)[None, :]
    shared["m1"] = np.where(jj <= ii, 0.0, NEG50).astype(np.float32)
    shared["m2"] = np.where(jj > ii, 0.0, NEG50).astype(np.float32)
    shared["ident"] = np.eye(128, dtype=BF)

    in_maps = []
    for c in range(8):
        m = dict(shared)
        m["x"] = np.ascontiguousarray(x[c], dtype=np.float32)
        in_maps.append(m)
    return in_maps


def kernel(**inputs):
    in_maps = _host_layout(inputs)
    nc = _build_program()
    res = run_bass_kernel_spmd(nc, in_maps, core_ids=list(range(8)))
    out = np.stack([np.asarray(r["out"]) for r in res.results])
    return out.astype(np.float32)
